# version 16
# baseline (speedup 1.0000x reference)
import contextlib
import numpy as np
import concourse.bass as bass
import concourse.mybir as mybir
from concourse.bass_utils import run_bass_kernel_spmd

F32 = mybir.dt.float32
BF16 = mybir.dt.bfloat16
I32 = mybir.dt.int32
AF = mybir.ActivationFunctionType
ALU = mybir.AluOpType
AX = mybir.AxisListType

S = 2048
D = 2048
NH = 16
FH = 8192
CH = 64
TA = 512
TF = 512
NTA = S // TA
NTF = S // TF
NTB = TA // 128
NDC = D // 128
NFC = FH // 128
CPT = TA // CH
NCHUNK = S // CH
LN_EPS = 1e-5
RMS_EPS = 1e-8

ENGS = ("sync", "scalar", "tensor", "vector", "gpsimd")


class Sem:
    def __init__(self, stack, nc, name):
        self.h = stack.enter_context(nc.semaphore(name))
        self.n = 0


class K:
    def __init__(self):
        self.items = {e: [] for e in ENGS}

    def on(self, eng, fn):
        self.items[eng].append(fn)

    def waits(self, eng, evs):
        best = {}
        for ev in evs:
            if ev is None:
                continue
            sem, val = ev
            key = id(sem)
            if key not in best or val > best[key][1]:
                best[key] = (sem, val)
        for sem, val in best.values():
            if val > 0:
                self.on(eng, lambda e, s=sem, v=val: e.wait_ge(s, v))

    def drain(self, eng):
        self.on(eng, lambda e: e.drain())


def build(dbg=False):
    nc = bass.Bass("TRN2", target_bir_lowering=False)
    DK = "ExternalOutput" if dbg else "Internal"  

    x_h = nc.dram_tensor("x", [S, D], F32, kind="ExternalInput").ap()
    wq_h = nc.dram_tensor("wq", [D, D], F32, kind="ExternalInput").ap()
    wk_h = nc.dram_tensor("wk", [D, D], F32, kind="ExternalInput").ap()
    wb_h = nc.dram_tensor("wb", [D, D], F32, kind="ExternalInput").ap()
    wv_h = nc.dram_tensor("wv", [D, D], F32, kind="ExternalInput").ap()
    wo_h = nc.dram_tensor("wo", [D, D], F32, kind="ExternalInput").ap()
    wgt_h = nc.dram_tensor("wgt", [D, FH], F32, kind="ExternalInput").ap()
    wut_h = nc.dram_tensor("wut", [D, FH], F32, kind="ExternalInput").ap()
    wdt_h = nc.dram_tensor("wdt", [FH, D], F32, kind="ExternalInput").ap()
    xt_h = nc.dram_tensor("xt", [D, S], F32, kind="ExternalInput").ap()
    mask_h = nc.dram_tensor("maskb", [128, 512], F32, kind="ExternalInput").ap()
    y_h = nc.dram_tensor("y", [S, D], F32, kind="ExternalOutput").ap()

    swx = {nm: nc.dram_tensor("s" + nm, [D, D], BF16, kind="Internal").ap()
           for nm in ("wq", "wk", "wb", "wv", "wo")}
    swgT = nc.dram_tensor("swgT", [D, FH], BF16, kind="Internal").ap()
    swuT = nc.dram_tensor("swuT", [D, FH], BF16, kind="Internal").ap()
    swdT = nc.dram_tensor("swdT", [FH, D], BF16, kind="Internal").ap()
    sq = nc.dram_tensor("sq", [TA, D], BF16, kind=DK).ap()
    sk = nc.dram_tensor("sk", [TA, D], BF16, kind=DK).ap()
    sstream = nc.dram_tensor("sstream", [S, D], BF16, kind=DK).ap()
    sxq = nc.dram_tensor("sxq", [S, D], BF16, kind=DK).ap()
    dbg_p = nc.dram_tensor("dbg_p", [128, 32768], BF16, kind=DK).ap() if dbg else None
    dbg_xqf = nc.dram_tensor("dbg_xqf", [128, 8192], BF16, kind=DK).ap() if dbg else None

    stack = contextlib.ExitStack()
    with stack:
        sb = lambda name, shape, dt, **kw: stack.enter_context(
            nc.sbuf_tensor(name, shape, dt, **kw))
        psb = lambda name: stack.enter_context(
            nc.psum_tensor(name, [128, 512], F32))

        ph1 = [sb(f"ph1_{i}", [128, 8192], BF16, is_transpose_dest=True)
               for i in range(4)]
        ph2 = [sb(f"ph2_{i}", [128, 16384], BF16) for i in range(2)]
        ph3a = sb("ph3a", [128, 8192], BF16, is_transpose_dest=True)
        ph3b = sb("ph3b", [128, 8192], BF16)
        state_sb = sb("state_sb", [128, NH * 128], BF16)
        lnin = sb("lnin", [128, D], F32)
        outst = sb("outst", [128, D], F32)
        tmpf = sb("tmpf", [128, D], F32)
        maskb = sb("mask_sb", [128, 512], F32)
        stats = sb("stats", [128, 96], F32)
        wdb = [sb(f"wdb_{i}", [128, 512], BF16, is_transpose_dest=True)
               for i in range(4)]


        x_fm, q_fm, k_fm, k_tok = ph1
        u_tok = ph3a
        xq_fm = ph3a
        PP0, PP1, SS = 0, 2048, 4096
        AT0, AT1 = 6144, 6656
        stg = ph3b[:, SS:SS + 2048]

        pb = [psb(f"pb{i}") for i in range(8)]

        sT = Sem(stack, nc, "sT")
        sV = Sem(stack, nc, "sV")
        sA = Sem(stack, nc, "sA")

        k = K()

        class Q:

            def __init__(self, eng, name, nsem):
                self.eng = eng
                self.sems = [Sem(stack, nc, f"{name}{i}") for i in range(nsem)]
                self.i = 0

            def batch(self, items, accum=False, transpose=False):
                sm = self.sems[self.i % len(self.sems)]
                self.i += 1
                k.waits(self.eng, [(sm.h, sm.n)])
                for out_ap, in_ap in items:
                    sm.n += 16
                    if transpose:
                        k.on(self.eng, lambda e, o=out_ap, i=in_ap, s=sm.h:
                             e.dma_start(o, i, transpose=True).then_inc(s, 16))
                    elif accum:
                        k.on(self.eng, lambda e, o=out_ap, i=in_ap, s=sm.h:
                             e.dma_start(o, i, accum_op=ALU.add)
                             .then_inc(s, 16))
                    else:
                        k.on(self.eng, lambda e, o=out_ap, i=in_ap, s=sm.h:
                             e.dma_start(o, i).then_inc(s, 16))
                return (sm.h, sm.n)

            def qevents(self):
                return [(sm.h, sm.n) for sm in self.sems if sm.n > 0]

        gq = Q("gpsimd", "qg", 6)
        syq = Q("sync", "qs", 6)
        scq = Q("scalar", "qa", 8)
        y_evs = []

        def t_inc(fn):
            sT.n += 1
            k.on("tensor", lambda e, f=fn, s=sT.h: f(e).then_inc(s, 1))
            return (sT.h, sT.n)

        def v_inc(fn):
            sV.n += 1
            k.on("vector", lambda e, f=fn, s=sV.h: f(e).then_inc(s, 1))
            return (sV.h, sV.n)

        def a_inc(fn):
            sA.n += 1
            k.on("scalar", lambda e, f=fn, s=sA.h: f(e).then_inc(s, 1))
            return (sA.h, sA.n)

        def t_quiesce():
            sT.n += 1
            k.on("tensor", lambda e, s=sT.h: e.drain().then_inc(s, 1))
            return (sT.h, sT.n)

        def mm_group(bank_ap, pairs, extra_wait=()):
            k.waits("tensor", list(extra_wait))
            n = len(pairs)
            for j in range(n - 1):
                l, r = pairs[j]
                k.on("tensor", lambda e, o=bank_ap, ll=l, rr=r, a=(j == 0):
                     e.matmul(o, ll, rr, start=a, stop=False,
                              skip_group_check=True))
            l, r = pairs[-1]
            return t_inc(lambda e, o=bank_ap, ll=l, rr=r, a=(n == 1):
                         e.matmul(o, ll, rr, start=a, stop=True,
                                  skip_group_check=True))

        def ln_stats(src_fn, eps_ap, pre_waits_v, base=0):
            b = base
            k.waits("vector", pre_waits_v)
            k.drain("vector")
            for i in range(4):
                k.on("vector", lambda e, i_=i, b_=b: e.bn_stats(
                    stats[:, b_ + i_ * 6:b_ + (i_ + 1) * 6], src_fn(i_)))
            k.drain("vector")
            ev_st = v_inc(lambda e, b_=b: e.bn_aggr(
                stats[:, b_ + 24:b_ + 26], stats[:, b_:b_ + 24]))
            k.waits("scalar", [ev_st])
            ev_sq = a_inc(lambda e, b_=b: e.activation(
                stats[:, b_ + 29:b_ + 30], stats[:, b_ + 25:b_ + 26],
                AF.Sqrt, bias=eps_ap))
            k.waits("vector", [ev_sq])
            k.on("vector", lambda e, b_=b: e.reciprocal(
                stats[:, b_ + 26:b_ + 27], stats[:, b_ + 29:b_ + 30]))
            k.drain("vector")
            k.on("vector", lambda e, b_=b: e.tensor_tensor(
                stats[:, b_ + 27:b_ + 28], stats[:, b_ + 24:b_ + 25],
                stats[:, b_ + 26:b_ + 27], op=ALU.mult))
            k.drain("vector")
            ev_nmi = v_inc(lambda e, b_=b: e.tensor_scalar_mul(
                stats[:, b_ + 28:b_ + 29], stats[:, b_ + 27:b_ + 28], -1.0))
            return ev_nmi, stats[:, b + 26:b + 27], stats[:, b + 28:b + 29]

        k.on("vector", lambda e: e.memset(stats[:, 88:89], LN_EPS))
        k.on("vector", lambda e: e.memset(stats[:, 89:90], RMS_EPS))
        eps_ln = stats[:, 88:89]
        eps_rms = stats[:, 89:90]
        ev_mask = syq.batch([(maskb[:, :], mask_h[:, :])])
        attnw_hbm = {"wq": wq_h, "wk": wk_h, "wb": wb_h, "wv": wv_h,
                     "wo": wo_h}
        swx_ev = {}
        swx_ev["wq"] = gq.batch([(swx["wq"][:, :], attnw_hbm["wq"][:, :])])
        swx_ev["wk"] = gq.batch([(swx["wk"][:, :], attnw_hbm["wk"][:, :])])
        for nm in ("wb", "wv", "wo"):
            swx_ev[nm] = gq.batch([(swx[nm][:, :], attnw_hbm[nm][:, :])])

        strips = []
        for w_hh, s_hh, nrow in ((wgt_h, swgT, D), (wut_h, swuT, D),
                                 (wdt_h, swdT, FH)):
            nr = nrow // 128
            nc_ = (FH if nrow == D else D) // 2048
            for r in range(nr):
                for c in range(nc_):
                    strips.append(
                        (w_hh[r * 128:(r + 1) * 128, c * 2048:(c + 1) * 2048],
                         s_hh[r * 128:(r + 1) * 128, c * 2048:(c + 1) * 2048]))
        N_STRIPS = len(strips)
        pp = {"i": 0, "sign_ev": [None, None], "out_ev": [None, None]}
        pp_outs = [None] * N_STRIPS
        pp_off = [PP0, PP1]

        def pump(n):
            for _ in range(n):
                i = pp["i"]
                if i >= N_STRIPS:
                    return
                pp["i"] = i + 1
                par = i % 2
                off = pp_off[par]
                src, dst = strips[i]
                k.waits("gpsimd", [pp["out_ev"][par]])
                ev_in = gq.batch([(ph3b[:, off:off + 2048], src)])
                k.waits("scalar", [ev_in])
                ev_sg = a_inc(lambda e, o=off: e.sign(
                    ph3b[:, o:o + 2048], ph3b[:, o:o + 2048]))
                pp["sign_ev"][par] = ev_sg
                k.waits("scalar", [ev_sg])
                pp["out_ev"][par] = scq.batch([(dst,
                                                 ph3b[:, off:off + 2048])])
                pp_outs[i] = pp["out_ev"][par]

        wh_consumed = [None, None]

        def load_half(par, nm, half):
            k.waits("scalar", [wh_consumed[par], swx_ev[nm]])
            items = []
            for gblk in range(4):
                o3 = ph2[par][:, gblk * 4096:(gblk + 1) * 4096].rearrange(
                    "p (a c) -> p a c", a=4)
                i3 = swx[nm][gblk * 512:(gblk + 1) * 512,
                             half * 1024:(half + 1) * 1024]
                i3t = i3.rearrange("(a q) c -> a q c", a=4).transpose([1, 0, 2])
                items.append((o3, i3t))
            return scq.batch(items)

        HW = ["wq", "wk", "wb", "wv", "wo"]
        half_sched = [(t, widx, half) for t in range(NTA)
                      for widx in range(5) for half in (0, 1)]
        half_evs, half_par = {}, {}
        hs = {"i": 0}

        def issue_halves(n):
            for _ in range(n):
                i = hs["i"]
                if i >= len(half_sched):
                    return
                hs["i"] = i + 1
                t, widx, half = half_sched[i]
                par = i % 2
                half_evs[(t, widx, half)] = load_half(par, HW[widx], half)
                half_par[(t, widx, half)] = par

        rot_free = [None] * 8
        stage_ev = [None] * 4
        beta_ev = {}
        att_sb_free = [None, None]
        attbank_free = [None, None]
        obank_free = [None, None]
        state_evac_ev = [None] * 4
        state_mm_ev = [None] * 4
        lnin_free = None
        outst_free = None
        ln1_free = [None, None]
        stg_free = [None, None]
        ln1_stage1_ev = None
        ln1_stage1_guard = None
        xfm_free = None
        qfm_free = None
        kfm_free = None
        ktok_ufree = None
        prev_qtrr = None
        prev_ktrr = None
        stream_done = [None] * NTA
        ev_q_trr = [None] * NH
        ev_k_trr = [None] * NH

        issue_halves(2)

        def proj_tm(t, widx, evac):
            for half in (0, 1):
                par = half_par[(t, widx, half)]
                lev = half_evs[(t, widx, half)]
                for fgl in range(2):
                    fg = half * 2 + fgl
                    for tb in range(NTB):
                        bi = 4 + (tb % 2) * 2 + fgl
                        pairs = []
                        for dc in range(NDC):
                            lhsT = x_fm[:, dc * 512 + tb * 128:
                                        dc * 512 + (tb + 1) * 128]
                            rhs = ph2[par][:, dc * 1024 + fgl * 512:
                                           dc * 1024 + (fgl + 1) * 512]
                            pairs.append((lhsT, rhs))
                        ev = mm_group(pb[bi][:, :], pairs,
                                      extra_wait=[lev, rot_free[bi]])
                        rot_free[bi] = evac(tb, fg, bi, ev)
                wh_consumed[par] = t_quiesce()
                issue_halves(1)

        def emit_quant_tb(f, tb, use_wdb=False):
            frow = f * TF
            geb = None if use_wdb else ph3a[:, 0:2048]
            k.waits("gpsimd", stream_done[f] + [self_outst_free[0]])
            ld_ev = gq.batch([(outst[:, :],
                               sstream[frow + tb * 128:
                                       frow + (tb + 1) * 128, :])])
            ev_nmi, inv_ap, nmi_ap = ln_stats(
                lambda i: outst[:, i * 512:(i + 1) * 512],
                eps_rms, [ld_ev, self_lnin_free[0], self_tmpf_free[0],
                          self_geb_gate[0]], base=0)
            k.waits("scalar", [ev_nmi, self_lnin_free[0]])
            ev_rm = a_inc(lambda e, sc_=inv_ap, bi_=nmi_ap: e.activation(
                lnin[:, :], outst[:, :], AF.Identity, bias=bi_, scale=sc_))
            k.waits("vector", [ev_rm])
            k.on("vector", lambda e: e.tensor_reduce(
                stats[:, 80:81], lnin[:, :], axis=AX.X, op=ALU.max,
                apply_absolute_value=True))
            k.drain("vector")
            k.on("vector", lambda e: e.reciprocal(stats[:, 81:82],
                                                  stats[:, 80:81]))
            k.drain("vector")
            k.on("vector", lambda e: e.tensor_scalar_mul(
                stats[:, 82:83], stats[:, 81:82], 127.0))
            k.on("vector", lambda e: e.tensor_scalar_mul(
                stats[:, 83:84], stats[:, 80:81], 1.0 / 127.0))
            k.drain("vector")
            ev_qs = v_inc(lambda e: e.tensor_scalar_mul(
                outst[:, :], lnin[:, :], stats[:, 82:83]))
            k.waits("scalar", [ev_qs])
            k.on("scalar", lambda e: e.sign(lnin[:, :], outst[:, :]))
            ev_ab = a_inc(lambda e: e.activation(tmpf[:, :], outst[:, :],
                                                 AF.Abs))
            k.waits("vector", [ev_ab])
            k.on("vector", lambda e: e.tensor_scalar_add(
                tmpf[:, :], tmpf[:, :], 0.5))
            k.drain("vector")
            k.on("vector", lambda e: e.tensor_copy(outst[:, :], tmpf[:, :]))
            k.drain("vector")
            for kbit in (64.0, 32.0, 16.0, 8.0, 4.0, 2.0, 1.0):
                if use_wdb:
                    for j in range(4):
                        k.on("vector", lambda e, kb=kbit, j_=j:
                             e.tensor_scalar(
                                 wdb[j_][:, :],
                                 tmpf[:, j_ * 512:(j_ + 1) * 512], kb, kb,
                                 op0=ALU.is_ge, op1=ALU.mult))
                    k.drain("vector")
                    for j in range(4):
                        k.on("vector", lambda e, j_=j: e.tensor_tensor(
                            tmpf[:, j_ * 512:(j_ + 1) * 512],
                            tmpf[:, j_ * 512:(j_ + 1) * 512],
                            wdb[j_][:, :], op=ALU.subtract))
                    k.drain("vector")
                else:
                    k.on("vector", lambda e, kb=kbit: e.tensor_scalar(
                        geb, tmpf[:, :], kb, kb, op0=ALU.is_ge, op1=ALU.mult))
                    k.drain("vector")
                    k.on("vector", lambda e: e.tensor_tensor(
                        tmpf[:, :], tmpf[:, :], geb, op=ALU.subtract))
                    k.drain("vector")
            k.on("vector", lambda e: e.tensor_tensor(
                tmpf[:, :], outst[:, :], tmpf[:, :], op=ALU.subtract))
            k.drain("vector")
            k.on("vector", lambda e: e.tensor_scalar_min(
                tmpf[:, :], tmpf[:, :], 127.0))
            k.drain("vector")
            ev_xq = v_inc(lambda e: e.scalar_tensor_tensor(
                outst[:, :], lnin[:, :], stats[:, 83:84], tmpf[:, :],
                op0=ALU.mult, op1=ALU.mult))
            self_lnin_free[0] = ev_xq
            self_tmpf_free[0] = ev_xq
            self_outst_free[0] = ev_xq
            k.waits("gpsimd", [ev_xq])
            wr = gq.batch([(sxq[frow + tb * 128:frow + (tb + 1) * 128, :],
                            outst[:, :])])
            self_outst_free[0] = wr
            return wr, (sV.h, sV.n)

        def emit_xq_trr(f, xq_wr, extra):
            frow = f * TF
            k.waits("sync", xq_wr + extra)
            return syq.batch(
                [(xq_fm[:, dc * 512:(dc + 1) * 512],
                  sxq[frow:frow + TF, dc * 128:(dc + 1) * 128])
                 for dc in range(NDC)], transpose=True)

        self_outst_free = [None]
        self_lnin_free = [None]
        self_tmpf_free = [None]
        self_geb_gate = [None]

        self_lnin_free[0] = ln1_free[0]
        self_tmpf_free[0] = ln1_free[1]

        hoisted_xfm = None
        hq_wr = []
        hq_done = None
        for t in range(NTA):
            trow = t * TA

            if hoisted_xfm is not None:
                ev_xfm = hoisted_xfm
            else:
                k.waits("gpsimd", [xfm_free])
                ev_xfm = gq.batch(
                    [(x_fm[:, dc * 512:(dc + 1) * 512],
                      xt_h[dc * 128:(dc + 1) * 128, trow:trow + TA])
                     for dc in range(NDC)])

            def q_evac(tb, fg, bi, mm_ev):
                j = (tb * 4 + fg) % 4
                sl = ph3b[:, SS + j * 512:SS + (j + 1) * 512]
                k.waits("vector", [mm_ev, stage_ev[j]])
                v_ev = v_inc(lambda e, o=sl, i=pb[bi][:, :]:
                             e.tensor_copy(o, i))
                k.waits("scalar", [v_ev, prev_qtrr])
                stage_ev[j] = scq.batch(
                    [(sq[tb * 128:(tb + 1) * 128,
                         fg * 512:(fg + 1) * 512], sl)])
                return v_ev

            k.waits("tensor", [ev_xfm])
            proj_tm(t, 0, q_evac)
            if t >= 2:
                self_lnin_free[0] = ln1_free[0]
                self_tmpf_free[0] = ln1_free[1]
                for tb_h in ((0, 1) if t == 2 else (2, 3)):
                    wr, hq_done = emit_quant_tb(0, tb_h, use_wdb=True)
                    hq_wr.append(wr)
                ln1_free[0] = hq_done
                ln1_free[1] = hq_done
            q_spills = list(stage_ev)
            pump(10)

            k.waits("sync", q_spills + [qfm_free])
            ev_qtrr_all = syq.batch(
                [(q_fm[:, h * 512:(h + 1) * 512], sq[:, h * 128:(h + 1) * 128])
                 for h in range(NH)], transpose=True)
            prev_qtrr = ev_qtrr_all

            def k_evac(tb, fg, bi, mm_ev):
                sl = k_tok[:, tb * 2048 + fg * 512:tb * 2048 + (fg + 1) * 512]
                k.waits("vector", [mm_ev, ktok_ufree])
                v_ev = v_inc(lambda e, o=sl, i=pb[bi][:, :]:
                             e.tensor_copy(o, i))
                k.waits("scalar", [v_ev, prev_ktrr])
                k_spill_evs.append(scq.batch(
                    [(sk[tb * 128:(tb + 1) * 128,
                         fg * 512:(fg + 1) * 512], sl)]))
                return v_ev

            k_spill_evs = []
            proj_tm(t, 1, k_evac)
            pump(10)

            k.waits("sync", k_spill_evs + [kfm_free])
            ev_ktrr_all = syq.batch(
                [(k_fm[:, h * 512:(h + 1) * 512], sk[:, h * 128:(h + 1) * 128])
                 for h in range(NH)], transpose=True)
            prev_ktrr = ev_ktrr_all

            def b_evac(tb, fg, bi, mm_ev):
                sl = u_tok[:, tb * 2048 + fg * 512:tb * 2048 + (fg + 1) * 512]
                k.waits("scalar", [mm_ev, ktok_ufree])
                ev = a_inc(lambda e, o=sl, i=pb[bi][:, :]:
                           e.activation(o, i, AF.Sigmoid))
                beta_ev[(tb, fg)] = ev
                return ev

            proj_tm(t, 2, b_evac)
            pump(10)

            def v_evac(tb, fg, bi, mm_ev):
                sl = u_tok[:, tb * 2048 + fg * 512:tb * 2048 + (fg + 1) * 512]
                k.waits("vector", [mm_ev, beta_ev[(tb, fg)]])
                return v_inc(lambda e, o=sl, i=pb[bi][:, :]:
                             e.tensor_tensor(o, o, i, op=ALU.mult))

            proj_tm(t, 3, v_evac)
            u_done = (sV.h, sV.n)
            xfm_free = t_quiesce()
            pump(10)
            if t + 1 < NTA:
                k.waits("gpsimd", [xfm_free])
                hoisted_xfm = gq.batch(
                    [(x_fm[:, dc * 512:(dc + 1) * 512],
                      xt_h[dc * 128:(dc + 1) * 128,
                           (t + 1) * TA:(t + 2) * TA])
                     for dc in range(NDC)])

            for c in range(CPT):
                g = t * CPT + c
                boff = (c % 2) * 64
                tb = c // 2
                mask_evs = [None, None]
                for hb in range(2):
                    bi = 4 + hb
                    k.waits("tensor", [attbank_free[hb], rot_free[4 + hb],
                                       ev_ktrr_all, ev_qtrr_all])
                    p1_ev = None
                    for hh in range(8):
                        h = hb * 8 + hh
                        l = k_fm[:, h * 512 + c * 64:h * 512 + (c + 1) * 64]
                        r = q_fm[:, h * 512 + c * 64:h * 512 + (c + 1) * 64]
                        o = pb[bi][boff:boff + 64, hh * 64:(hh + 1) * 64]
                        p1_ev = t_inc(lambda e, oo=o, ll=l, rr=r: e.matmul(
                            oo, ll, rr, start=True, stop=True,
                            skip_group_check=True))
                    ato = AT0 if hb == 0 else AT1
                    k.waits("vector", [p1_ev, att_sb_free[hb], ev_mask,
                                       ln1_stage1_ev])
                    mask_evs[hb] = v_inc(
                        lambda e, o=ph3b[boff:boff + 64, ato:ato + 512],
                        i=pb[bi][boff:boff + 64, :],
                        m=maskb[boff:boff + 64, :]:
                        e.tensor_tensor(o, i, m, op=ALU.mult))
                    attbank_free[hb] = mask_evs[hb]
                for hb in range(2):
                    obi = 6 + hb
                    ato = AT0 if hb == 0 else AT1
                    k.waits("tensor", [mask_evs[hb], obank_free[hb],
                                       rot_free[6 + hb], u_done])
                    last_o = None
                    for hh in range(8):
                        h = hb * 8 + hh
                        sbk = h // 4
                        u_l = u_tok[boff:boff + 64,
                                    tb * 2048 + h * 128:
                                    tb * 2048 + (h + 1) * 128]
                        at_r = ph3b[boff:boff + 64,
                                    ato + hh * 64:ato + (hh + 1) * 64]
                        o_ap = pb[obi][:, hh * 64:(hh + 1) * 64]
                        k.waits("tensor", [state_evac_ev[sbk]])
                        if g > 0:
                            k.on("tensor", lambda e, oo=o_ap, ll=u_l, rr=at_r:
                                 e.matmul(oo, ll, rr, start=True, stop=False,
                                          skip_group_check=True))
                            st_l = state_sb[:, h * 128:(h + 1) * 128]
                            q_r = q_fm[:, h * 512 + c * 64:
                                       h * 512 + (c + 1) * 64]
                            last_o = t_inc(
                                lambda e, oo=o_ap, ll=st_l, rr=q_r: e.matmul(
                                    oo, ll, rr, start=False, stop=True,
                                    skip_group_check=True))
                        else:
                            last_o = t_inc(
                                lambda e, oo=o_ap, ll=u_l, rr=at_r: e.matmul(
                                    oo, ll, rr, start=True, stop=True,
                                    skip_group_check=True))
                        kt_l = k_tok[boff:boff + 64,
                                     tb * 2048 + h * 128:
                                     tb * 2048 + (h + 1) * 128]
                        st_o = pb[sbk][:, (h % 4) * 128:(h % 4 + 1) * 128]
                        a0 = (g == 0 and h % 4 == 0)
                        b0 = (g == NCHUNK - 1 and h % 4 == 3)
                        if h % 4 == 3:
                            state_mm_ev[sbk] = t_inc(
                                lambda e, oo=st_o, ll=kt_l, rr=u_l, a=a0,
                                b=b0: e.matmul(oo, ll, rr, start=a, stop=b,
                                               skip_group_check=True))
                        else:
                            k.on("tensor",
                                 lambda e, oo=st_o, ll=kt_l, rr=u_l, a=a0,
                                 b=b0: e.matmul(oo, ll, rr, start=a, stop=b,
                                                skip_group_check=True))
                    k.waits("vector", [last_o])
                    oq = q_fm[:, :].rearrange("p (h s) -> p h s", h=NH)
                    opv = pb[obi][:, :].rearrange("p (h s) -> p h s", h=8)
                    oev = v_inc(lambda e,
                                o=oq[:, hb * 8:(hb + 1) * 8,
                                     c * 64:(c + 1) * 64],
                                i=opv: e.tensor_copy(o, i))
                    obank_free[hb] = oev
                chunk_q = t_quiesce()
                att_sb_free[0] = chunk_q
                att_sb_free[1] = chunk_q
                if g < NCHUNK - 1:
                    for sbk in range(4):
                        k.waits("vector", [state_mm_ev[sbk]])
                        state_evac_ev[sbk] = v_inc(
                            lambda e,
                            o=state_sb[:, sbk * 512:(sbk + 1) * 512],
                            i=pb[sbk][:, :]: e.tensor_copy(o, i))
            scan_o_done = (sV.h, sV.n)
            scan_mm_done = chunk_q
            ktok_ufree = scan_mm_done
            kfm_free = scan_mm_done
            pump(8)

            stream_evs = []
            xld = [None] * NTB
            for tb in (0, 1):
                par2 = tb % 2
                buf = lnin if par2 == 0 else tmpf
                k.waits("scalar", [ln1_free[par2]])
                xld[tb] = scq.batch(
                    [(buf[:, :],
                      x_h[trow + tb * 128:trow + (tb + 1) * 128, :])])
            for tb in range(NTB):
                par2 = tb % 2
                buf = lnin if par2 == 0 else tmpf
                stgb = stg if par2 == 0 else ph3b[:, 6144:8192]
                if xld[tb] is None:
                    k.waits("scalar", [ln1_free[par2]])
                    xld[tb] = scq.batch(
                        [(buf[:, :],
                          x_h[trow + tb * 128:trow + (tb + 1) * 128, :])])
                evac_evs = []
                for dg in range(4):
                    half = dg // 2
                    dgl = dg % 2
                    par = half_par[(t, 4, half)]
                    lev = half_evs[(t, 4, half)]
                    bi = 4 + (tb % 2) * 2 + dgl
                    pairs = []
                    for fc in range(NDC):
                        lhsT = q_fm[:, fc * 512 + tb * 128:
                                    fc * 512 + (tb + 1) * 128]
                        rhs = ph2[par][:, fc * 1024 + dgl * 512:
                                       fc * 1024 + (dgl + 1) * 512]
                        pairs.append((lhsT, rhs))
                    ev = mm_group(pb[bi][:, :], pairs,
                                  extra_wait=[lev, rot_free[bi], scan_o_done])
                    k.waits("vector", [ev, xld[tb]])
                    eev = v_inc(lambda e, o=buf[:, dg * 512:(dg + 1) * 512],
                                i=pb[bi][:, :]: e.tensor_tensor(
                                    o, o, i, op=ALU.add))
                    rot_free[bi] = eev
                    evac_evs.append(eev)
                ev_nmi, inv_ap, nmi_ap = ln_stats(
                    lambda i, b_=buf: b_[:, i * 512:(i + 1) * 512], eps_ln,
                    evac_evs, base=36 * par2)
                k.waits("scalar", [ev_nmi, stg_free[par2]] +
                        (stage_ev if par2 == 0 else []))
                ap_ev = a_inc(lambda e, o=stgb, b_=buf, sc_=inv_ap,
                              bi_=nmi_ap: e.activation(
                                  o, b_[:, :], AF.Identity, bias=bi_,
                                  scale=sc_))
                ln1_free[par2] = ap_ev
                k.waits("scalar", [ap_ev])
                stg_free[par2] = scq.batch(
                    [(sstream[trow + tb * 128:trow + (tb + 1) * 128, :],
                      stgb)])
                stream_evs.append(stg_free[par2])
            stream_done[t] = stream_evs
            for j in range(4):
                stage_ev[j] = stg_free[0]
            ln1_stage1_ev = stg_free[1]
            wo_q = t_quiesce()
            qfm_free = wo_q
            wh_consumed[half_par[(t, 4, 0)]] = wo_q
            wh_consumed[half_par[(t, 4, 1)]] = wo_q
            issue_halves(2)
            pump(8)

        pump(N_STRIPS)
        tdone_list = scq.qevents()
        ppd_list = tdone_list
        attn_mm = wo_q
        attn_v = (sV.h, sV.n)
        attn_a = (sA.h, sA.n)

        p_sl = lambda fc: ph2[fc // 32][:, (fc % 32) * 512:(fc % 32 + 1) * 512]
        stream_sl = lambda tb: ph3b[:, tb * 2048:(tb + 1) * 2048]

        span_free = [None, None]
        dbg_evs = []
        quant_init = False
        wdb_free = [None] * 4
        y_ev = None
        prev_ln2_use = None
        prev_xq_trr = None
        prev_xq_wr = None
        prev_u_done = None
        prev_wd_read = None
        ln2_usage = None

        for f in range(NTF):
            frow = f * TF

            if f == 0:
                self_geb_gate[0] = attn_mm
                ev_xq_trr = emit_xq_trr(0, hq_wr, [attn_mm, hq_done])

            stream_ld = [None] * NTB
            for tb in range(NTB):
                k.waits("gpsimd", stream_done[f] +
                        [prev_wd_read, prev_ln2_use] +
                        (tdone_list + [attn_v, attn_mm, attn_a]
                         if f == 0 else []))
                stream_ld[tb] = gq.batch(
                    [(stream_sl(tb),
                      sstream[frow + tb * 128:frow + (tb + 1) * 128, :])])

            def load_span(par, w_sT, sidx, first_pair, phase_extra=None):
                extra = []
                if f == 0 and first_pair:
                    extra.extend(tdone_list)
                if first_pair:
                    extra.append(prev_u_done)
                    if phase_extra is not None:
                        extra.append(phase_extra)
                a, b = ph1[par * 2], ph1[par * 2 + 1]
                k.waits("sync", [span_free[par]] + extra)
                k.waits("scalar", [span_free[par]] + extra)
                it_s, it_a = [], []
                for dc in range(NDC):
                    dst = a if dc < 8 else b
                    dsl = dst[:, (dc % 8) * 1024:(dc % 8 + 1) * 1024]
                    src = w_sT[dc * 128:(dc + 1) * 128,
                               sidx * 1024:(sidx + 1) * 1024]
                    (it_s if dc % 2 == 0 else it_a).append((dsl, src))
                evs = syq.batch(it_s)
                evsc = scq.batch(it_a)
                return (evs, evsc)

            gu_rot = [None, None, None]

            def gu_phase(w_s, is_g, phase_extra=None):
                for sidx in range(8):
                    par = sidx % 2
                    lev = load_span(par, w_s, sidx, sidx < 2, phase_extra)
                    a, b = ph1[par * 2], ph1[par * 2 + 1]
                    last = None
                    for jfc in range(8):
                        fc = sidx * 8 + jfc
                        bi = jfc % 3
                        pairs = []
                        for dc in range(NDC):
                            src = a if dc < 8 else b
                            lhsT = src[:, (dc % 8) * 1024 + jfc * 128:
                                       (dc % 8) * 1024 + (jfc + 1) * 128]
                            rhs = xq_fm[:, dc * 512:(dc + 1) * 512]
                            pairs.append((lhsT, rhs))
                        ev = mm_group(pb[bi][:, :], pairs,
                                      extra_wait=[lev[0], lev[1], gu_rot[bi],
                                                  ev_xq_trr])
                        if is_g:
                            k.waits("scalar", [ev])
                            gu_rot[bi] = a_inc(
                                lambda e, o=p_sl(fc), i=pb[bi][:, :]:
                                e.activation(o, i, AF.Silu))
                        else:
                            k.waits("vector", [ev, gu_rot[bi]])
                            gu_rot[bi] = v_inc(
                                lambda e, o=p_sl(fc), i=pb[bi][:, :]:
                                e.tensor_tensor(o, o, i, op=ALU.mult))
                        last = ev
                    span_free[par] = t_quiesce()

            gu_phase(swgT, True)
            g_done_a = (sA.h, sA.n)
            gu_phase(swuT, False, phase_extra=g_done_a)
            u_done_v = (sV.h, sV.n)
            u_mm_done = t_quiesce()
            prev_u_done = u_mm_done
            if f + 1 < NTF:
                self_geb_gate[0] = u_mm_done
                next_xq_wr = []
            if dbg and f == NTF - 1:
                k.waits("scalar", [u_done_v, ev_xq_trr])
                dbg_evs.append(scq.batch([(dbg_p[:, 0:16384], ph2[0][:, :])]))
                dbg_evs.append(scq.batch([(dbg_p[:, 16384:32768],
                                           ph2[1][:, :])]))
                dbg_evs.append(scq.batch([(dbg_xqf[:, :], ph3a[:, :])]))

            for dg in range(4):
                for fc in range(NFC):
                    par4 = fc % 4
                    qeng = "sync" if fc % 2 == 0 else "scalar"
                    qq = syq if fc % 2 == 0 else scq
                    k.waits(qeng, [wdb_free[par4]])
                    lev = qq.batch(
                        [(wdb[par4][:, :],
                          swdT[fc * 128:(fc + 1) * 128,
                               dg * 512:(dg + 1) * 512])])
                    wm_extra = [lev, u_done_v, g_done_a] if fc == 0 else [lev]
                    k.waits("tensor", wm_extra)
                    for tb in range(NTB):
                        bi = 4 + tb
                        if fc == 0:
                            k.waits("tensor", [rot_free[bi]])
                        o = pb[bi][:, :]
                        l = p_sl(fc)[:, tb * 128:(tb + 1) * 128]
                        r = wdb[par4][:, :]
                        ev = t_inc(lambda e, oo=o, ll=l, rr=r,
                                   a=(fc == 0), b=(fc == NFC - 1):
                                   e.matmul(oo, ll, rr, start=a, stop=b,
                                            skip_group_check=True))
                        if tb == NTB - 1:
                            wdb_free[par4] = ev
                wd_done = (sT.h, sT.n)
                for tb in range(NTB):
                    bi = 4 + tb
                    k.waits("vector", [wd_done, u_mm_done, u_done_v,
                                       stream_ld[tb]])
                    ev = v_inc(lambda e,
                               o=stream_sl(tb)[:, dg * 512:(dg + 1) * 512],
                               i=pb[bi][:, :]:
                               e.tensor_tensor(o, o, i, op=ALU.add))
                    rot_free[bi] = ev
                if f + 1 < NTF:
                    wr, qd = emit_quant_tb(f + 1, dg)
                    next_xq_wr.append(wr)
                    next_quant_done = qd
            ln2_ready = (sV.h, sV.n)
            prev_wd_read = ln2_ready
            if f + 1 < NTF:
                next_xq_trr = emit_xq_trr(f + 1, next_xq_wr,
                                          [next_quant_done])

            for tb in range(NTB):
                par2 = tb % 2
                buf = outst if par2 == 0 else tmpf
                buf_free = self_outst_free if par2 == 0 else self_tmpf_free
                ev_nmi, inv_ap, nmi_ap = ln_stats(
                    lambda i, tb_=tb: stream_sl(tb_)[:, i * 512:(i + 1) * 512],
                    eps_ln, [ln2_ready, buf_free[0]], base=36 * par2)
                k.waits("scalar", [ev_nmi, buf_free[0]])
                ap_ev = a_inc(lambda e, tb_=tb, o=buf, sc_=inv_ap,
                              bi_=nmi_ap: e.activation(
                                  o[:, :], stream_sl(tb_), AF.Identity,
                                  bias=bi_, scale=sc_))
                ln2_usage = ap_ev
                prev_ln2_use = ap_ev
                buf_free[0] = ap_ev
                k.waits("scalar", [ap_ev])
                y_ev = scq.batch(
                    [(y_h[frow + tb * 128:frow + (tb + 1) * 128, :],
                      buf[:, :])])
                y_evs.append(y_ev)
                buf_free[0] = y_ev
            if f + 1 < NTF:
                ev_xq_trr = next_xq_trr

        k.waits("sync", y_evs + dbg_evs)

        with nc.Block() as block:
            @block.sync
            def _(e):
                for fn in k.items["sync"]:
                    fn(e)

            @block.scalar
            def _(e):
                for fn in k.items["scalar"]:
                    fn(e)

            @block.tensor
            def _(e):
                for fn in k.items["tensor"]:
                    fn(e)

            @block.vector
            def _(e):
                for fn in k.items["vector"]:
                    fn(e)

            @block.gpsimd
            def _(e):
                for fn in k.items["gpsimd"]:
                    fn(e)

    return nc


_MASK = None


def _mask_const():
    global _MASK
    if _MASK is None:
        tri = np.triu(np.ones((64, 64), dtype=np.float32))
        _MASK = np.ascontiguousarray(np.tile(tri, (2, 8)))
    return _MASK


def kernel(L0_attn_hidden_states, L0_attn_Wq, L0_attn_Wk, L0_attn_Wv,
           L0_attn_Wb, L0_attn_Wo, L0_ffn_W_g, L0_ffn_W_u, L0_ffn_W_d):
    x = np.ascontiguousarray(np.asarray(L0_attn_hidden_states,
                                        dtype=np.float32))
    ws = {
        "wq": np.ascontiguousarray(np.asarray(L0_attn_Wq, dtype=np.float32)),
        "wk": np.ascontiguousarray(np.asarray(L0_attn_Wk, dtype=np.float32)),
        "wv": np.ascontiguousarray(np.asarray(L0_attn_Wv, dtype=np.float32)),
        "wb": np.ascontiguousarray(np.asarray(L0_attn_Wb, dtype=np.float32)),
        "wo": np.ascontiguousarray(np.asarray(L0_attn_Wo, dtype=np.float32)),
        "wgt": np.ascontiguousarray(np.asarray(L0_ffn_W_g, dtype=np.float32).T),
        "wut": np.ascontiguousarray(np.asarray(L0_ffn_W_u, dtype=np.float32).T),
        "wdt": np.ascontiguousarray(np.asarray(L0_ffn_W_d, dtype=np.float32).T),
        "maskb": _mask_const(),
    }
    nc = build()
    in_maps = []
    for b in range(8):
        m = {"x": np.ascontiguousarray(x[b]),
             "xt": np.ascontiguousarray(x[b].T)}
        m.update(ws)
        in_maps.append(m)
    res = run_bass_kernel_spmd(nc, in_maps, core_ids=list(range(8)))
    return np.stack([r["y"] for r in res.results], axis=0)


# version 17
# speedup vs baseline: 1.0024x; 1.0024x over previous
import contextlib
import numpy as np
import concourse.bass as bass
import concourse.mybir as mybir
from concourse.bass_utils import run_bass_kernel_spmd

F32 = mybir.dt.float32
BF16 = mybir.dt.bfloat16
I32 = mybir.dt.int32
AF = mybir.ActivationFunctionType
ALU = mybir.AluOpType
AX = mybir.AxisListType

S = 2048
D = 2048
NH = 16
FH = 8192
CH = 64
TA = 512
TF = 512
NTA = S // TA
NTF = S // TF
NTB = TA // 128
NDC = D // 128
NFC = FH // 128
CPT = TA // CH
NCHUNK = S // CH
LN_EPS = 1e-5
RMS_EPS = 1e-8

ENGS = ("sync", "scalar", "tensor", "vector", "gpsimd")


class Sem:
    def __init__(self, stack, nc, name):
        self.h = stack.enter_context(nc.semaphore(name))
        self.n = 0


class K:
    def __init__(self):
        self.items = {e: [] for e in ENGS}

    def on(self, eng, fn):
        self.items[eng].append(fn)

    def waits(self, eng, evs):
        best = {}
        for ev in evs:
            if ev is None:
                continue
            sem, val = ev
            key = id(sem)
            if key not in best or val > best[key][1]:
                best[key] = (sem, val)
        for sem, val in best.values():
            if val > 0:
                self.on(eng, lambda e, s=sem, v=val: e.wait_ge(s, v))

    def drain(self, eng):
        self.on(eng, lambda e: e.drain())


def build(dbg=False):
    nc = bass.Bass("TRN2", target_bir_lowering=False)
    DK = "ExternalOutput" if dbg else "Internal"  

    x_h = nc.dram_tensor("x", [S, D], F32, kind="ExternalInput").ap()
    wq_h = nc.dram_tensor("wq", [D, D], F32, kind="ExternalInput").ap()
    wk_h = nc.dram_tensor("wk", [D, D], F32, kind="ExternalInput").ap()
    wb_h = nc.dram_tensor("wb", [D, D], F32, kind="ExternalInput").ap()
    wv_h = nc.dram_tensor("wv", [D, D], F32, kind="ExternalInput").ap()
    wo_h = nc.dram_tensor("wo", [D, D], F32, kind="ExternalInput").ap()
    wgt_h = nc.dram_tensor("wgt", [D, FH], F32, kind="ExternalInput").ap()
    wut_h = nc.dram_tensor("wut", [D, FH], F32, kind="ExternalInput").ap()
    wdt_h = nc.dram_tensor("wdt", [FH, D], F32, kind="ExternalInput").ap()
    xt_h = nc.dram_tensor("xt", [D, S], F32, kind="ExternalInput").ap()
    mask_h = nc.dram_tensor("maskb", [128, 512], F32, kind="ExternalInput").ap()
    y_h = nc.dram_tensor("y", [S, D], F32, kind="ExternalOutput").ap()

    swx = {nm: nc.dram_tensor("s" + nm, [D, D], BF16, kind="Internal").ap()
           for nm in ("wq", "wk", "wb", "wv", "wo")}
    swgT = nc.dram_tensor("swgT", [D, FH], BF16, kind="Internal").ap()
    swuT = nc.dram_tensor("swuT", [D, FH], BF16, kind="Internal").ap()
    swdT = nc.dram_tensor("swdT", [FH, D], BF16, kind="Internal").ap()
    sq = nc.dram_tensor("sq", [TA, D], BF16, kind=DK).ap()
    sk = nc.dram_tensor("sk", [TA, D], BF16, kind=DK).ap()
    sstream = nc.dram_tensor("sstream", [S, D], BF16, kind=DK).ap()
    sxq = nc.dram_tensor("sxq", [S, D], BF16, kind=DK).ap()
    dbg_p = nc.dram_tensor("dbg_p", [128, 32768], BF16, kind=DK).ap() if dbg else None
    dbg_xqf = nc.dram_tensor("dbg_xqf", [128, 8192], BF16, kind=DK).ap() if dbg else None

    stack = contextlib.ExitStack()
    with stack:
        sb = lambda name, shape, dt, **kw: stack.enter_context(
            nc.sbuf_tensor(name, shape, dt, **kw))
        psb = lambda name: stack.enter_context(
            nc.psum_tensor(name, [128, 512], F32))

        ph1 = [sb(f"ph1_{i}", [128, 8192], BF16, is_transpose_dest=True)
               for i in range(4)]
        ph2 = [sb(f"ph2_{i}", [128, 16384], BF16) for i in range(2)]
        ph3a = sb("ph3a", [128, 8192], BF16, is_transpose_dest=True)
        ph3b = sb("ph3b", [128, 8192], BF16)
        state_sb = sb("state_sb", [128, NH * 128], BF16)
        lnin = sb("lnin", [128, D], F32)
        outst = sb("outst", [128, D], F32)
        tmpf = sb("tmpf", [128, D], F32)
        maskb = sb("mask_sb", [128, 512], F32)
        stats = sb("stats", [128, 96], F32)
        wdb = [sb(f"wdb_{i}", [128, 512], BF16, is_transpose_dest=True)
               for i in range(4)]


        x_fm, q_fm, k_fm, k_tok = ph1
        u_tok = ph3a
        xq_fm = ph3a
        PP0, PP1, SS = 0, 2048, 4096
        AT0, AT1 = 6144, 6656
        stg = ph3b[:, SS:SS + 2048]

        pb = [psb(f"pb{i}") for i in range(8)]

        sT = Sem(stack, nc, "sT")
        sV = Sem(stack, nc, "sV")
        sA = Sem(stack, nc, "sA")

        k = K()

        class Q:

            def __init__(self, eng, name, nsem):
                self.eng = eng
                self.sems = [Sem(stack, nc, f"{name}{i}") for i in range(nsem)]
                self.i = 0

            def batch(self, items, accum=False, transpose=False):
                sm = self.sems[self.i % len(self.sems)]
                self.i += 1
                k.waits(self.eng, [(sm.h, sm.n)])
                for out_ap, in_ap in items:
                    sm.n += 16
                    if transpose:
                        k.on(self.eng, lambda e, o=out_ap, i=in_ap, s=sm.h:
                             e.dma_start(o, i, transpose=True).then_inc(s, 16))
                    elif accum:
                        k.on(self.eng, lambda e, o=out_ap, i=in_ap, s=sm.h:
                             e.dma_start(o, i, accum_op=ALU.add)
                             .then_inc(s, 16))
                    else:
                        k.on(self.eng, lambda e, o=out_ap, i=in_ap, s=sm.h:
                             e.dma_start(o, i).then_inc(s, 16))
                return (sm.h, sm.n)

            def qevents(self):
                return [(sm.h, sm.n) for sm in self.sems if sm.n > 0]

        gq = Q("gpsimd", "qg", 6)
        syq = Q("sync", "qs", 6)
        scq = Q("scalar", "qa", 8)
        y_evs = []

        def t_inc(fn):
            sT.n += 1
            k.on("tensor", lambda e, f=fn, s=sT.h: f(e).then_inc(s, 1))
            return (sT.h, sT.n)

        def v_inc(fn):
            sV.n += 1
            k.on("vector", lambda e, f=fn, s=sV.h: f(e).then_inc(s, 1))
            return (sV.h, sV.n)

        def a_inc(fn):
            sA.n += 1
            k.on("scalar", lambda e, f=fn, s=sA.h: f(e).then_inc(s, 1))
            return (sA.h, sA.n)

        def t_quiesce():
            sT.n += 1
            k.on("tensor", lambda e, s=sT.h: e.drain().then_inc(s, 1))
            return (sT.h, sT.n)

        def mm_group(bank_ap, pairs, extra_wait=()):
            k.waits("tensor", list(extra_wait))
            n = len(pairs)
            for j in range(n - 1):
                l, r = pairs[j]
                k.on("tensor", lambda e, o=bank_ap, ll=l, rr=r, a=(j == 0):
                     e.matmul(o, ll, rr, start=a, stop=False,
                              skip_group_check=True))
            l, r = pairs[-1]
            return t_inc(lambda e, o=bank_ap, ll=l, rr=r, a=(n == 1):
                         e.matmul(o, ll, rr, start=a, stop=True,
                                  skip_group_check=True))

        def ln_stats(src_fn, eps_ap, pre_waits_v, base=0):
            b = base
            k.waits("vector", pre_waits_v)
            k.drain("vector")
            for i in range(4):
                k.on("vector", lambda e, i_=i, b_=b: e.bn_stats(
                    stats[:, b_ + i_ * 6:b_ + (i_ + 1) * 6], src_fn(i_)))
            k.drain("vector")
            ev_st = v_inc(lambda e, b_=b: e.bn_aggr(
                stats[:, b_ + 24:b_ + 26], stats[:, b_:b_ + 24]))
            k.waits("scalar", [ev_st])
            ev_sq = a_inc(lambda e, b_=b: e.activation(
                stats[:, b_ + 29:b_ + 30], stats[:, b_ + 25:b_ + 26],
                AF.Sqrt, bias=eps_ap))
            k.waits("vector", [ev_sq])
            k.on("vector", lambda e, b_=b: e.reciprocal(
                stats[:, b_ + 26:b_ + 27], stats[:, b_ + 29:b_ + 30]))
            k.drain("vector")
            k.on("vector", lambda e, b_=b: e.tensor_tensor(
                stats[:, b_ + 27:b_ + 28], stats[:, b_ + 24:b_ + 25],
                stats[:, b_ + 26:b_ + 27], op=ALU.mult))
            k.drain("vector")
            ev_nmi = v_inc(lambda e, b_=b: e.tensor_scalar_mul(
                stats[:, b_ + 28:b_ + 29], stats[:, b_ + 27:b_ + 28], -1.0))
            return ev_nmi, stats[:, b + 26:b + 27], stats[:, b + 28:b + 29]

        k.on("vector", lambda e: e.memset(stats[:, 88:89], LN_EPS))
        k.on("vector", lambda e: e.memset(stats[:, 89:90], RMS_EPS))
        eps_ln = stats[:, 88:89]
        eps_rms = stats[:, 89:90]
        ev_mask = syq.batch([(maskb[:, :], mask_h[:, :])])
        attnw_hbm = {"wq": wq_h, "wk": wk_h, "wb": wb_h, "wv": wv_h,
                     "wo": wo_h}
        swx_ev = {}
        swx_ev["wq"] = gq.batch([(swx["wq"][:, :], attnw_hbm["wq"][:, :])])
        swx_ev["wk"] = gq.batch([(swx["wk"][:, :], attnw_hbm["wk"][:, :])])
        for nm in ("wb", "wv", "wo"):
            swx_ev[nm] = gq.batch([(swx[nm][:, :], attnw_hbm[nm][:, :])])

        strips = []
        for w_hh, s_hh, nrow in ((wgt_h, swgT, D), (wut_h, swuT, D),
                                 (wdt_h, swdT, FH)):
            nr = nrow // 128
            nc_ = (FH if nrow == D else D) // 2048
            for r in range(nr):
                for c in range(nc_):
                    strips.append(
                        (w_hh[r * 128:(r + 1) * 128, c * 2048:(c + 1) * 2048],
                         s_hh[r * 128:(r + 1) * 128, c * 2048:(c + 1) * 2048]))
        N_STRIPS = len(strips)
        pp = {"i": 0, "sign_ev": [None, None], "out_ev": [None, None]}
        pp_outs = [None] * N_STRIPS
        pp_off = [PP0, PP1]

        def pump(n):
            for _ in range(n):
                i = pp["i"]
                if i >= N_STRIPS:
                    return
                pp["i"] = i + 1
                par = i % 2
                off = pp_off[par]
                src, dst = strips[i]
                k.waits("gpsimd", [pp["out_ev"][par]])
                ev_in = gq.batch([(ph3b[:, off:off + 2048], src)])
                k.waits("scalar", [ev_in])
                ev_sg = a_inc(lambda e, o=off: e.sign(
                    ph3b[:, o:o + 2048], ph3b[:, o:o + 2048]))
                pp["sign_ev"][par] = ev_sg
                k.waits("scalar", [ev_sg])
                pp["out_ev"][par] = scq.batch([(dst,
                                                 ph3b[:, off:off + 2048])])
                pp_outs[i] = pp["out_ev"][par]

        wh_consumed = [None, None]

        def load_half(par, nm, half):
            k.waits("scalar", [wh_consumed[par], swx_ev[nm]])
            items = []
            for gblk in range(4):
                o3 = ph2[par][:, gblk * 4096:(gblk + 1) * 4096].rearrange(
                    "p (a c) -> p a c", a=4)
                i3 = swx[nm][gblk * 512:(gblk + 1) * 512,
                             half * 1024:(half + 1) * 1024]
                i3t = i3.rearrange("(a q) c -> a q c", a=4).transpose([1, 0, 2])
                items.append((o3, i3t))
            return scq.batch(items)

        HW = ["wq", "wk", "wb", "wv", "wo"]
        half_sched = [(t, widx, half) for t in range(NTA)
                      for widx in range(5) for half in (0, 1)]
        half_evs, half_par = {}, {}
        hs = {"i": 0}

        def issue_halves(n):
            for _ in range(n):
                i = hs["i"]
                if i >= len(half_sched):
                    return
                hs["i"] = i + 1
                t, widx, half = half_sched[i]
                par = i % 2
                half_evs[(t, widx, half)] = load_half(par, HW[widx], half)
                half_par[(t, widx, half)] = par

        rot_free = [None] * 8
        stage_ev = [None] * 4
        beta_ev = {}
        att_sb_free = [None, None]
        attbank_free = [None, None]
        obank_free = [None, None]
        state_evac_ev = [None] * 4
        state_mm_ev = [None] * 4
        lnin_free = None
        outst_free = None
        ln1_free = [None, None]
        stg_free = [None, None]
        ln1_stage1_ev = None
        ln1_stage1_guard = None
        xfm_free = None
        qfm_free = None
        kfm_free = None
        ktok_ufree = None
        prev_qtrr = None
        prev_ktrr = None
        stream_done = [None] * NTA
        ev_q_trr = [None] * NH
        ev_k_trr = [None] * NH

        issue_halves(2)

        def proj_tm(t, widx, evac):
            for half in (0, 1):
                par = half_par[(t, widx, half)]
                lev = half_evs[(t, widx, half)]
                for fgl in range(2):
                    fg = half * 2 + fgl
                    for tb in range(NTB):
                        bi = 4 + (tb % 2) * 2 + fgl
                        pairs = []
                        for dc in range(NDC):
                            lhsT = x_fm[:, dc * 512 + tb * 128:
                                        dc * 512 + (tb + 1) * 128]
                            rhs = ph2[par][:, dc * 1024 + fgl * 512:
                                           dc * 1024 + (fgl + 1) * 512]
                            pairs.append((lhsT, rhs))
                        ev = mm_group(pb[bi][:, :], pairs,
                                      extra_wait=[lev, rot_free[bi]])
                        rot_free[bi] = evac(tb, fg, bi, ev)
                wh_consumed[par] = t_quiesce()
                issue_halves(1)

        def emit_quant_tb(f, tb, use_wdb=False):
            frow = f * TF
            geb = None if use_wdb else ph3a[:, 0:2048]
            k.waits("gpsimd", stream_done[f] + [self_outst_free[0]])
            ld_ev = gq.batch([(outst[:, :],
                               sstream[frow + tb * 128:
                                       frow + (tb + 1) * 128, :])])
            ev_nmi, inv_ap, nmi_ap = ln_stats(
                lambda i: outst[:, i * 512:(i + 1) * 512],
                eps_rms, [ld_ev, self_lnin_free[0], self_tmpf_free[0],
                          self_geb_gate[0]], base=0)
            k.waits("scalar", [ev_nmi, self_lnin_free[0]])
            ev_rm = a_inc(lambda e, sc_=inv_ap, bi_=nmi_ap: e.activation(
                lnin[:, :], outst[:, :], AF.Identity, bias=bi_, scale=sc_))
            k.waits("vector", [ev_rm])
            k.on("vector", lambda e: e.tensor_reduce(
                stats[:, 80:81], lnin[:, :], axis=AX.X, op=ALU.max,
                apply_absolute_value=True))
            k.drain("vector")
            k.on("vector", lambda e: e.reciprocal(stats[:, 81:82],
                                                  stats[:, 80:81]))
            k.drain("vector")
            k.on("vector", lambda e: e.tensor_scalar_mul(
                stats[:, 82:83], stats[:, 81:82], 127.0))
            k.on("vector", lambda e: e.tensor_scalar_mul(
                stats[:, 83:84], stats[:, 80:81], 1.0 / 127.0))
            k.drain("vector")
            ev_qs = v_inc(lambda e: e.tensor_scalar_mul(
                outst[:, :], lnin[:, :], stats[:, 82:83]))
            k.waits("scalar", [ev_qs])
            k.on("scalar", lambda e: e.sign(lnin[:, :], outst[:, :]))
            ev_ab = a_inc(lambda e: e.activation(tmpf[:, :], outst[:, :],
                                                 AF.Abs))
            k.waits("vector", [ev_ab])
            k.on("vector", lambda e: e.tensor_scalar_add(
                tmpf[:, :], tmpf[:, :], 0.5))
            k.drain("vector")
            k.on("vector", lambda e: e.tensor_copy(outst[:, :], tmpf[:, :]))
            k.drain("vector")
            for kbit in (64.0, 32.0, 16.0, 8.0, 4.0, 2.0, 1.0):
                if use_wdb:
                    for j in range(4):
                        k.on("vector", lambda e, kb=kbit, j_=j:
                             e.tensor_scalar(
                                 wdb[j_][:, :],
                                 tmpf[:, j_ * 512:(j_ + 1) * 512], kb, kb,
                                 op0=ALU.is_ge, op1=ALU.mult))
                    k.drain("vector")
                    for j in range(4):
                        k.on("vector", lambda e, j_=j: e.tensor_tensor(
                            tmpf[:, j_ * 512:(j_ + 1) * 512],
                            tmpf[:, j_ * 512:(j_ + 1) * 512],
                            wdb[j_][:, :], op=ALU.subtract))
                    k.drain("vector")
                else:
                    k.on("vector", lambda e, kb=kbit: e.tensor_scalar(
                        geb, tmpf[:, :], kb, kb, op0=ALU.is_ge, op1=ALU.mult))
                    k.drain("vector")
                    k.on("vector", lambda e: e.tensor_tensor(
                        tmpf[:, :], tmpf[:, :], geb, op=ALU.subtract))
                    k.drain("vector")
            k.on("vector", lambda e: e.tensor_tensor(
                tmpf[:, :], outst[:, :], tmpf[:, :], op=ALU.subtract))
            k.drain("vector")
            k.on("vector", lambda e: e.tensor_scalar_min(
                tmpf[:, :], tmpf[:, :], 127.0))
            k.drain("vector")
            ev_xq = v_inc(lambda e: e.scalar_tensor_tensor(
                outst[:, :], lnin[:, :], stats[:, 83:84], tmpf[:, :],
                op0=ALU.mult, op1=ALU.mult))
            self_lnin_free[0] = ev_xq
            self_tmpf_free[0] = ev_xq
            self_outst_free[0] = ev_xq
            k.waits("gpsimd", [ev_xq])
            wr = gq.batch([(sxq[frow + tb * 128:frow + (tb + 1) * 128, :],
                            outst[:, :])])
            self_outst_free[0] = wr
            return wr, (sV.h, sV.n)

        def emit_xq_trr(f, xq_wr, extra):
            frow = f * TF
            k.waits("sync", xq_wr + extra)
            return syq.batch(
                [(xq_fm[:, dc * 512:(dc + 1) * 512],
                  sxq[frow:frow + TF, dc * 128:(dc + 1) * 128])
                 for dc in range(NDC)], transpose=True)

        self_outst_free = [None]
        self_lnin_free = [None]
        self_tmpf_free = [None]
        self_geb_gate = [None]

        self_lnin_free[0] = ln1_free[0]
        self_tmpf_free[0] = ln1_free[1]

        hoisted_xfm = None
        hq_wr = []
        hq_done = None
        for t in range(NTA):
            trow = t * TA

            if hoisted_xfm is not None:
                ev_xfm = hoisted_xfm
            else:
                k.waits("gpsimd", [xfm_free])
                ev_xfm = gq.batch(
                    [(x_fm[:, dc * 512:(dc + 1) * 512],
                      xt_h[dc * 128:(dc + 1) * 128, trow:trow + TA])
                     for dc in range(NDC)])

            def q_evac(tb, fg, bi, mm_ev):
                j = (tb * 4 + fg) % 4
                sl = ph3b[:, SS + j * 512:SS + (j + 1) * 512]
                k.waits("vector", [mm_ev, stage_ev[j]])
                v_ev = v_inc(lambda e, o=sl, i=pb[bi][:, :]:
                             e.tensor_copy(o, i))
                k.waits("scalar", [v_ev, prev_qtrr])
                stage_ev[j] = scq.batch(
                    [(sq[tb * 128:(tb + 1) * 128,
                         fg * 512:(fg + 1) * 512], sl)])
                return v_ev

            k.waits("tensor", [ev_xfm])
            proj_tm(t, 0, q_evac)
            if t >= 2:
                self_lnin_free[0] = ln1_free[0]
                self_tmpf_free[0] = ln1_free[1]
                for tb_h in ((0, 1) if t == 2 else (2, 3)):
                    wr, hq_done = emit_quant_tb(0, tb_h, use_wdb=True)
                    hq_wr.append(wr)
                ln1_free[0] = hq_done
                ln1_free[1] = hq_done
            q_spills = list(stage_ev)
            pump(10)

            k.waits("sync", q_spills + [qfm_free])
            ev_qtrr_all = syq.batch(
                [(q_fm[:, h * 512:(h + 1) * 512], sq[:, h * 128:(h + 1) * 128])
                 for h in range(NH)], transpose=True)
            prev_qtrr = ev_qtrr_all

            def k_evac(tb, fg, bi, mm_ev):
                sl = k_tok[:, tb * 2048 + fg * 512:tb * 2048 + (fg + 1) * 512]
                k.waits("vector", [mm_ev, ktok_ufree])
                v_ev = v_inc(lambda e, o=sl, i=pb[bi][:, :]:
                             e.tensor_copy(o, i))
                k.waits("scalar", [v_ev, prev_ktrr])
                k_spill_evs.append(scq.batch(
                    [(sk[tb * 128:(tb + 1) * 128,
                         fg * 512:(fg + 1) * 512], sl)]))
                return v_ev

            k_spill_evs = []
            proj_tm(t, 1, k_evac)
            pump(10)

            k.waits("sync", k_spill_evs + [kfm_free])
            ev_ktrr_all = syq.batch(
                [(k_fm[:, h * 512:(h + 1) * 512], sk[:, h * 128:(h + 1) * 128])
                 for h in range(NH)], transpose=True)
            prev_ktrr = ev_ktrr_all

            def b_evac(tb, fg, bi, mm_ev):
                sl = u_tok[:, tb * 2048 + fg * 512:tb * 2048 + (fg + 1) * 512]
                k.waits("scalar", [mm_ev, ktok_ufree])
                ev = a_inc(lambda e, o=sl, i=pb[bi][:, :]:
                           e.activation(o, i, AF.Sigmoid))
                beta_ev[(tb, fg)] = ev
                return ev

            proj_tm(t, 2, b_evac)
            pump(10)

            def v_evac(tb, fg, bi, mm_ev):
                sl = u_tok[:, tb * 2048 + fg * 512:tb * 2048 + (fg + 1) * 512]
                k.waits("vector", [mm_ev, beta_ev[(tb, fg)]])
                return v_inc(lambda e, o=sl, i=pb[bi][:, :]:
                             e.tensor_tensor(o, o, i, op=ALU.mult))

            proj_tm(t, 3, v_evac)
            u_done = (sV.h, sV.n)
            xfm_free = t_quiesce()
            pump(10)
            if t + 1 < NTA:
                k.waits("gpsimd", [xfm_free])
                hoisted_xfm = gq.batch(
                    [(x_fm[:, dc * 512:(dc + 1) * 512],
                      xt_h[dc * 128:(dc + 1) * 128,
                           (t + 1) * TA:(t + 2) * TA])
                     for dc in range(NDC)])

            for c in range(CPT):
                g = t * CPT + c
                boff = (c % 2) * 64
                tb = c // 2
                mask_evs = [None, None]
                for hb in range(2):
                    bi = 4 + hb
                    k.waits("tensor", [attbank_free[hb], rot_free[4 + hb],
                                       ev_ktrr_all, ev_qtrr_all])
                    p1_ev = None
                    for hh in range(8):
                        h = hb * 8 + hh
                        l = k_fm[:, h * 512 + c * 64:h * 512 + (c + 1) * 64]
                        r = q_fm[:, h * 512 + c * 64:h * 512 + (c + 1) * 64]
                        o = pb[bi][boff:boff + 64, hh * 64:(hh + 1) * 64]
                        p1_ev = t_inc(lambda e, oo=o, ll=l, rr=r: e.matmul(
                            oo, ll, rr, start=True, stop=True,
                            skip_group_check=True))
                    ato = AT0 if hb == 0 else AT1
                    k.waits("vector", [p1_ev, att_sb_free[hb], ev_mask,
                                       ln1_stage1_ev])
                    mask_evs[hb] = v_inc(
                        lambda e, o=ph3b[boff:boff + 64, ato:ato + 512],
                        i=pb[bi][boff:boff + 64, :],
                        m=maskb[boff:boff + 64, :]:
                        e.tensor_tensor(o, i, m, op=ALU.mult))
                    attbank_free[hb] = mask_evs[hb]
                for hb in range(2):
                    obi = 6 + hb
                    ato = AT0 if hb == 0 else AT1
                    k.waits("tensor", [mask_evs[hb], obank_free[hb],
                                       rot_free[6 + hb], u_done])
                    last_o = None
                    for hh in range(8):
                        h = hb * 8 + hh
                        sbk = h // 4
                        u_l = u_tok[boff:boff + 64,
                                    tb * 2048 + h * 128:
                                    tb * 2048 + (h + 1) * 128]
                        at_r = ph3b[boff:boff + 64,
                                    ato + hh * 64:ato + (hh + 1) * 64]
                        o_ap = pb[obi][:, hh * 64:(hh + 1) * 64]
                        k.waits("tensor", [state_evac_ev[sbk]])
                        if g > 0:
                            k.on("tensor", lambda e, oo=o_ap, ll=u_l, rr=at_r:
                                 e.matmul(oo, ll, rr, start=True, stop=False,
                                          skip_group_check=True))
                            st_l = state_sb[:, h * 128:(h + 1) * 128]
                            q_r = q_fm[:, h * 512 + c * 64:
                                       h * 512 + (c + 1) * 64]
                            last_o = t_inc(
                                lambda e, oo=o_ap, ll=st_l, rr=q_r: e.matmul(
                                    oo, ll, rr, start=False, stop=True,
                                    skip_group_check=True))
                        else:
                            last_o = t_inc(
                                lambda e, oo=o_ap, ll=u_l, rr=at_r: e.matmul(
                                    oo, ll, rr, start=True, stop=True,
                                    skip_group_check=True))
                        kt_l = k_tok[boff:boff + 64,
                                     tb * 2048 + h * 128:
                                     tb * 2048 + (h + 1) * 128]
                        st_o = pb[sbk][:, (h % 4) * 128:(h % 4 + 1) * 128]
                        a0 = (g == 0 and h % 4 == 0)
                        b0 = (g == NCHUNK - 1 and h % 4 == 3)
                        if h % 4 == 3:
                            state_mm_ev[sbk] = t_inc(
                                lambda e, oo=st_o, ll=kt_l, rr=u_l, a=a0,
                                b=b0: e.matmul(oo, ll, rr, start=a, stop=b,
                                               skip_group_check=True))
                        else:
                            k.on("tensor",
                                 lambda e, oo=st_o, ll=kt_l, rr=u_l, a=a0,
                                 b=b0: e.matmul(oo, ll, rr, start=a, stop=b,
                                                skip_group_check=True))
                    k.waits("vector", [last_o])
                    oq = q_fm[:, :].rearrange("p (h s) -> p h s", h=NH)
                    opv = pb[obi][:, :].rearrange("p (h s) -> p h s", h=8)
                    oev = v_inc(lambda e,
                                o=oq[:, hb * 8:(hb + 1) * 8,
                                     c * 64:(c + 1) * 64],
                                i=opv: e.tensor_copy(o, i))
                    obank_free[hb] = oev
                chunk_q = t_quiesce()
                att_sb_free[0] = chunk_q
                att_sb_free[1] = chunk_q
                if g < NCHUNK - 1:
                    for sbk in range(4):
                        k.waits("vector", [state_mm_ev[sbk]])
                        state_evac_ev[sbk] = v_inc(
                            lambda e,
                            o=state_sb[:, sbk * 512:(sbk + 1) * 512],
                            i=pb[sbk][:, :]: e.tensor_copy(o, i))
            scan_o_done = (sV.h, sV.n)
            scan_mm_done = chunk_q
            ktok_ufree = scan_mm_done
            kfm_free = scan_mm_done
            pump(8)

            stream_evs = []
            xld = [None] * NTB
            for tb in (0, 1):
                par2 = tb % 2
                buf = lnin if par2 == 0 else tmpf
                k.waits("scalar", [ln1_free[par2]])
                xld[tb] = scq.batch(
                    [(buf[:, :],
                      x_h[trow + tb * 128:trow + (tb + 1) * 128, :])])
            for tb in range(NTB):
                par2 = tb % 2
                buf = lnin if par2 == 0 else tmpf
                stgb = stg if par2 == 0 else ph3b[:, 6144:8192]
                if xld[tb] is None:
                    k.waits("scalar", [ln1_free[par2]])
                    xld[tb] = scq.batch(
                        [(buf[:, :],
                          x_h[trow + tb * 128:trow + (tb + 1) * 128, :])])
                evac_evs = []
                for dg in range(4):
                    half = dg // 2
                    dgl = dg % 2
                    par = half_par[(t, 4, half)]
                    lev = half_evs[(t, 4, half)]
                    bi = 4 + (tb % 2) * 2 + dgl
                    pairs = []
                    for fc in range(NDC):
                        lhsT = q_fm[:, fc * 512 + tb * 128:
                                    fc * 512 + (tb + 1) * 128]
                        rhs = ph2[par][:, fc * 1024 + dgl * 512:
                                       fc * 1024 + (dgl + 1) * 512]
                        pairs.append((lhsT, rhs))
                    ev = mm_group(pb[bi][:, :], pairs,
                                  extra_wait=[lev, rot_free[bi], scan_o_done])
                    k.waits("vector", [ev, xld[tb]])
                    eev = v_inc(lambda e, o=buf[:, dg * 512:(dg + 1) * 512],
                                i=pb[bi][:, :]: e.tensor_tensor(
                                    o, o, i, op=ALU.add))
                    rot_free[bi] = eev
                    evac_evs.append(eev)
                ev_nmi, inv_ap, nmi_ap = ln_stats(
                    lambda i, b_=buf: b_[:, i * 512:(i + 1) * 512], eps_ln,
                    evac_evs, base=36 * par2)
                k.waits("scalar", [ev_nmi, stg_free[par2]] +
                        (stage_ev if par2 == 0 else []))
                ap_ev = a_inc(lambda e, o=stgb, b_=buf, sc_=inv_ap,
                              bi_=nmi_ap: e.activation(
                                  o, b_[:, :], AF.Identity, bias=bi_,
                                  scale=sc_))
                ln1_free[par2] = ap_ev
                k.waits("scalar", [ap_ev])
                stg_free[par2] = scq.batch(
                    [(sstream[trow + tb * 128:trow + (tb + 1) * 128, :],
                      stgb)])
                stream_evs.append(stg_free[par2])
            stream_done[t] = stream_evs
            for j in range(4):
                stage_ev[j] = stg_free[0]
            ln1_stage1_ev = stg_free[1]
            wo_q = t_quiesce()
            qfm_free = wo_q
            wh_consumed[half_par[(t, 4, 0)]] = wo_q
            wh_consumed[half_par[(t, 4, 1)]] = wo_q
            issue_halves(2)
            pump(8)

        pump(N_STRIPS)
        tdone_list = scq.qevents()
        ppd_list = tdone_list
        attn_mm = wo_q
        attn_v = (sV.h, sV.n)
        attn_a = (sA.h, sA.n)

        p_sl = lambda fc: ph2[fc // 32][:, (fc % 32) * 512:(fc % 32 + 1) * 512]
        stream_sl = lambda tb: ph3b[:, tb * 2048:(tb + 1) * 2048]

        span_free = [None, None]
        pref_spans = {}
        dbg_evs = []
        quant_init = False
        wdb_free = [None] * 4
        y_ev = None
        prev_ln2_use = None
        prev_xq_trr = None
        prev_xq_wr = None
        prev_u_done = None
        prev_wd_read = None
        ln2_usage = None

        for f in range(NTF):
            frow = f * TF

            if f == 0:
                self_geb_gate[0] = attn_mm
                ev_xq_trr = emit_xq_trr(0, hq_wr, [attn_mm, hq_done])

            stream_ld = [None] * NTB
            for tb in range(NTB):
                k.waits("gpsimd", stream_done[f] +
                        [prev_wd_read, prev_ln2_use] +
                        (tdone_list + [attn_v, attn_mm, attn_a]
                         if f == 0 else []))
                stream_ld[tb] = gq.batch(
                    [(stream_sl(tb),
                      sstream[frow + tb * 128:frow + (tb + 1) * 128, :])])

            def load_span(par, w_sT, sidx, first_pair, phase_extra=None):
                extra = []
                if f == 0 and first_pair:
                    extra.extend(tdone_list)
                if first_pair:
                    extra.append(prev_u_done)
                    if phase_extra is not None:
                        extra.append(phase_extra)
                a, b = ph1[par * 2], ph1[par * 2 + 1]
                k.waits("sync", [span_free[par]] + extra)
                k.waits("scalar", [span_free[par]] + extra)
                it_s, it_a = [], []
                for dc in range(NDC):
                    dst = a if dc < 8 else b
                    dsl = dst[:, (dc % 8) * 1024:(dc % 8 + 1) * 1024]
                    src = w_sT[dc * 128:(dc + 1) * 128,
                               sidx * 1024:(sidx + 1) * 1024]
                    (it_s if dc % 2 == 0 else it_a).append((dsl, src))
                evs = syq.batch(it_s)
                evsc = scq.batch(it_a)
                return (evs, evsc)

            gu_rot = [None, None, None]

            def gu_phase(w_s, is_g, phase_extra=None):
                for sidx in range(8):
                    par = sidx % 2
                    if is_g and sidx < 2 and pref_spans.get(sidx) is not None:
                        lev = pref_spans.pop(sidx)
                    else:
                        lev = load_span(par, w_s, sidx, sidx < 2, phase_extra)
                    a, b = ph1[par * 2], ph1[par * 2 + 1]
                    last = None
                    for jfc in range(8):
                        fc = sidx * 8 + jfc
                        bi = jfc % 3
                        pairs = []
                        for dc in range(NDC):
                            src = a if dc < 8 else b
                            lhsT = src[:, (dc % 8) * 1024 + jfc * 128:
                                       (dc % 8) * 1024 + (jfc + 1) * 128]
                            rhs = xq_fm[:, dc * 512:(dc + 1) * 512]
                            pairs.append((lhsT, rhs))
                        ev = mm_group(pb[bi][:, :], pairs,
                                      extra_wait=[lev[0], lev[1], gu_rot[bi],
                                                  ev_xq_trr])
                        if is_g:
                            k.waits("scalar", [ev])
                            gu_rot[bi] = a_inc(
                                lambda e, o=p_sl(fc), i=pb[bi][:, :]:
                                e.activation(o, i, AF.Silu))
                        else:
                            k.waits("vector", [ev, gu_rot[bi]])
                            gu_rot[bi] = v_inc(
                                lambda e, o=p_sl(fc), i=pb[bi][:, :]:
                                e.tensor_tensor(o, o, i, op=ALU.mult))
                        last = ev
                    span_free[par] = t_quiesce()

            gu_phase(swgT, True)
            g_done_a = (sA.h, sA.n)
            gu_phase(swuT, False, phase_extra=g_done_a)
            u_done_v = (sV.h, sV.n)
            u_mm_done = t_quiesce()
            prev_u_done = u_mm_done
            if f + 1 < NTF:
                self_geb_gate[0] = u_mm_done
                next_xq_wr = []
            if dbg and f == NTF - 1:
                k.waits("scalar", [u_done_v, ev_xq_trr])
                dbg_evs.append(scq.batch([(dbg_p[:, 0:16384], ph2[0][:, :])]))
                dbg_evs.append(scq.batch([(dbg_p[:, 16384:32768],
                                           ph2[1][:, :])]))
                dbg_evs.append(scq.batch([(dbg_xqf[:, :], ph3a[:, :])]))

            for dg in range(4):
                for fc in range(NFC):
                    par4 = fc % 4
                    qeng = "sync" if fc % 2 == 0 else "scalar"
                    qq = syq if fc % 2 == 0 else scq
                    k.waits(qeng, [wdb_free[par4]])
                    lev = qq.batch(
                        [(wdb[par4][:, :],
                          swdT[fc * 128:(fc + 1) * 128,
                               dg * 512:(dg + 1) * 512])])
                    wm_extra = [lev, u_done_v, g_done_a] if fc == 0 else [lev]
                    k.waits("tensor", wm_extra)
                    for tb in range(NTB):
                        bi = 4 + tb
                        if fc == 0:
                            k.waits("tensor", [rot_free[bi]])
                        o = pb[bi][:, :]
                        l = p_sl(fc)[:, tb * 128:(tb + 1) * 128]
                        r = wdb[par4][:, :]
                        ev = t_inc(lambda e, oo=o, ll=l, rr=r,
                                   a=(fc == 0), b=(fc == NFC - 1):
                                   e.matmul(oo, ll, rr, start=a, stop=b,
                                            skip_group_check=True))
                        if tb == NTB - 1:
                            wdb_free[par4] = ev
                wd_done = (sT.h, sT.n)
                for tb in range(NTB):
                    bi = 4 + tb
                    k.waits("vector", [wd_done, u_mm_done, u_done_v,
                                       stream_ld[tb]])
                    ev = v_inc(lambda e,
                               o=stream_sl(tb)[:, dg * 512:(dg + 1) * 512],
                               i=pb[bi][:, :]:
                               e.tensor_tensor(o, o, i, op=ALU.add))
                    rot_free[bi] = ev
                if f + 1 < NTF:
                    wr, qd = emit_quant_tb(f + 1, dg)
                    next_xq_wr.append(wr)
                    next_quant_done = qd
            ln2_ready = (sV.h, sV.n)
            prev_wd_read = ln2_ready
            if f + 1 < NTF:
                next_xq_trr = emit_xq_trr(f + 1, next_xq_wr,
                                          [next_quant_done])
                for sidx in (0, 1):
                    pref_spans[sidx] = load_span(sidx % 2, swgT, sidx, True)

            for tb in range(NTB):
                par2 = tb % 2
                buf = outst if par2 == 0 else tmpf
                buf_free = self_outst_free if par2 == 0 else self_tmpf_free
                ev_nmi, inv_ap, nmi_ap = ln_stats(
                    lambda i, tb_=tb: stream_sl(tb_)[:, i * 512:(i + 1) * 512],
                    eps_ln, [ln2_ready, buf_free[0]], base=36 * par2)
                k.waits("scalar", [ev_nmi, buf_free[0]])
                ap_ev = a_inc(lambda e, tb_=tb, o=buf, sc_=inv_ap,
                              bi_=nmi_ap: e.activation(
                                  o[:, :], stream_sl(tb_), AF.Identity,
                                  bias=bi_, scale=sc_))
                ln2_usage = ap_ev
                prev_ln2_use = ap_ev
                buf_free[0] = ap_ev
                k.waits("scalar", [ap_ev])
                y_ev = scq.batch(
                    [(y_h[frow + tb * 128:frow + (tb + 1) * 128, :],
                      buf[:, :])])
                y_evs.append(y_ev)
                buf_free[0] = y_ev
            if f + 1 < NTF:
                ev_xq_trr = next_xq_trr

        k.waits("sync", y_evs + dbg_evs)

        with nc.Block() as block:
            @block.sync
            def _(e):
                for fn in k.items["sync"]:
                    fn(e)

            @block.scalar
            def _(e):
                for fn in k.items["scalar"]:
                    fn(e)

            @block.tensor
            def _(e):
                for fn in k.items["tensor"]:
                    fn(e)

            @block.vector
            def _(e):
                for fn in k.items["vector"]:
                    fn(e)

            @block.gpsimd
            def _(e):
                for fn in k.items["gpsimd"]:
                    fn(e)

    return nc


_MASK = None


def _mask_const():
    global _MASK
    if _MASK is None:
        tri = np.triu(np.ones((64, 64), dtype=np.float32))
        _MASK = np.ascontiguousarray(np.tile(tri, (2, 8)))
    return _MASK


def kernel(L0_attn_hidden_states, L0_attn_Wq, L0_attn_Wk, L0_attn_Wv,
           L0_attn_Wb, L0_attn_Wo, L0_ffn_W_g, L0_ffn_W_u, L0_ffn_W_d):
    x = np.ascontiguousarray(np.asarray(L0_attn_hidden_states,
                                        dtype=np.float32))
    ws = {
        "wq": np.ascontiguousarray(np.asarray(L0_attn_Wq, dtype=np.float32)),
        "wk": np.ascontiguousarray(np.asarray(L0_attn_Wk, dtype=np.float32)),
        "wv": np.ascontiguousarray(np.asarray(L0_attn_Wv, dtype=np.float32)),
        "wb": np.ascontiguousarray(np.asarray(L0_attn_Wb, dtype=np.float32)),
        "wo": np.ascontiguousarray(np.asarray(L0_attn_Wo, dtype=np.float32)),
        "wgt": np.ascontiguousarray(np.asarray(L0_ffn_W_g, dtype=np.float32).T),
        "wut": np.ascontiguousarray(np.asarray(L0_ffn_W_u, dtype=np.float32).T),
        "wdt": np.ascontiguousarray(np.asarray(L0_ffn_W_d, dtype=np.float32).T),
        "maskb": _mask_const(),
    }
    nc = build()
    in_maps = []
    for b in range(8):
        m = {"x": np.ascontiguousarray(x[b]),
             "xt": np.ascontiguousarray(x[b].T)}
        m.update(ws)
        in_maps.append(m)
    res = run_bass_kernel_spmd(nc, in_maps, core_ids=list(range(8)))
    return np.stack([r["y"] for r in res.results], axis=0)


# version 18
# speedup vs baseline: 1.0351x; 1.0326x over previous
import contextlib
import numpy as np
import concourse.bass as bass
import concourse.mybir as mybir
from concourse.bass_utils import run_bass_kernel_spmd

F32 = mybir.dt.float32
BF16 = mybir.dt.bfloat16
I32 = mybir.dt.int32
AF = mybir.ActivationFunctionType
ALU = mybir.AluOpType
AX = mybir.AxisListType

S = 2048
D = 2048
NH = 16
FH = 8192
CH = 64
TA = 512
TF = 512
NTA = S // TA
NTF = S // TF
NTB = TA // 128
NDC = D // 128
NFC = FH // 128
CPT = TA // CH
NCHUNK = S // CH
LN_EPS = 1e-5
RMS_EPS = 1e-8

ENGS = ("sync", "scalar", "tensor", "vector", "gpsimd")


class Sem:
    def __init__(self, stack, nc, name):
        self.h = stack.enter_context(nc.semaphore(name))
        self.n = 0


class K:
    def __init__(self):
        self.items = {e: [] for e in ENGS}

    def on(self, eng, fn):
        self.items[eng].append(fn)

    def waits(self, eng, evs):
        best = {}
        for ev in evs:
            if ev is None:
                continue
            sem, val = ev
            key = id(sem)
            if key not in best or val > best[key][1]:
                best[key] = (sem, val)
        for sem, val in best.values():
            if val > 0:
                self.on(eng, lambda e, s=sem, v=val: e.wait_ge(s, v))

    def drain(self, eng):
        self.on(eng, lambda e: e.drain())


def build(dbg=False):
    nc = bass.Bass("TRN2", target_bir_lowering=False)
    DK = "ExternalOutput" if dbg else "Internal"  

    x_h = nc.dram_tensor("x", [S, D], F32, kind="ExternalInput").ap()
    wq_h = nc.dram_tensor("wq", [D, D], F32, kind="ExternalInput").ap()
    wk_h = nc.dram_tensor("wk", [D, D], F32, kind="ExternalInput").ap()
    wb_h = nc.dram_tensor("wb", [D, D], F32, kind="ExternalInput").ap()
    wv_h = nc.dram_tensor("wv", [D, D], F32, kind="ExternalInput").ap()
    wo_h = nc.dram_tensor("wo", [D, D], F32, kind="ExternalInput").ap()
    wgt_h = nc.dram_tensor("wgt", [D, FH], F32, kind="ExternalInput").ap()
    wut_h = nc.dram_tensor("wut", [D, FH], F32, kind="ExternalInput").ap()
    wdt_h = nc.dram_tensor("wdt", [FH, D], F32, kind="ExternalInput").ap()
    xt_h = nc.dram_tensor("xt", [D, S], F32, kind="ExternalInput").ap()
    mask_h = nc.dram_tensor("maskb", [128, 512], F32, kind="ExternalInput").ap()
    y_h = nc.dram_tensor("y", [S, D], F32, kind="ExternalOutput").ap()

    swx = {nm: nc.dram_tensor("s" + nm, [D, D], BF16, kind="Internal").ap()
           for nm in ("wq", "wk", "wb", "wv", "wo")}
    swgT = nc.dram_tensor("swgT", [D, FH], BF16, kind="Internal").ap()
    swuT = nc.dram_tensor("swuT", [D, FH], BF16, kind="Internal").ap()
    swdT = nc.dram_tensor("swdT", [FH, D], BF16, kind="Internal").ap()
    sq = nc.dram_tensor("sq", [TA, D], BF16, kind=DK).ap()
    sk = nc.dram_tensor("sk", [TA, D], BF16, kind=DK).ap()
    sstream = nc.dram_tensor("sstream", [S, D], BF16, kind=DK).ap()
    sxq = nc.dram_tensor("sxq", [S, D], BF16, kind=DK).ap()
    dbg_p = nc.dram_tensor("dbg_p", [128, 32768], BF16, kind=DK).ap() if dbg else None
    dbg_xqf = nc.dram_tensor("dbg_xqf", [128, 8192], BF16, kind=DK).ap() if dbg else None

    stack = contextlib.ExitStack()
    with stack:
        sb = lambda name, shape, dt, **kw: stack.enter_context(
            nc.sbuf_tensor(name, shape, dt, **kw))
        psb = lambda name: stack.enter_context(
            nc.psum_tensor(name, [128, 512], F32))

        ph1 = [sb(f"ph1_{i}", [128, 8192], BF16, is_transpose_dest=True)
               for i in range(4)]
        ph2 = [sb(f"ph2_{i}", [128, 16384], BF16) for i in range(2)]
        ph3a = sb("ph3a", [128, 8192], BF16, is_transpose_dest=True)
        ph3b = sb("ph3b", [128, 8192], BF16)
        state_sb = sb("state_sb", [128, NH * 128], BF16)
        lnin = sb("lnin", [128, D], F32)
        outst = sb("outst", [128, D], F32)
        tmpf = sb("tmpf", [128, D], F32)
        maskb = sb("mask_sb", [128, 512], F32)
        stats = sb("stats", [128, 96], F32)
        wdb = [sb(f"wdb_{i}", [128, 512], BF16, is_transpose_dest=True)
               for i in range(4)]


        x_fm, q_fm, k_fm, k_tok = ph1
        u_tok = ph3a
        xq_fm = ph3a
        PP0, PP1, SS = 0, 2048, 4096
        AT0, AT1 = 6144, 6656
        stg = ph3b[:, SS:SS + 2048]

        pb = [psb(f"pb{i}") for i in range(8)]

        sT = Sem(stack, nc, "sT")
        sV = Sem(stack, nc, "sV")
        sA = Sem(stack, nc, "sA")

        k = K()

        class Q:

            def __init__(self, eng, name, nsem):
                self.eng = eng
                self.sems = [Sem(stack, nc, f"{name}{i}") for i in range(nsem)]
                self.i = 0

            def batch(self, items, accum=False, transpose=False):
                sm = self.sems[self.i % len(self.sems)]
                self.i += 1
                k.waits(self.eng, [(sm.h, sm.n)])
                for out_ap, in_ap in items:
                    sm.n += 16
                    if transpose:
                        k.on(self.eng, lambda e, o=out_ap, i=in_ap, s=sm.h:
                             e.dma_start(o, i, transpose=True).then_inc(s, 16))
                    elif accum:
                        k.on(self.eng, lambda e, o=out_ap, i=in_ap, s=sm.h:
                             e.dma_start(o, i, accum_op=ALU.add)
                             .then_inc(s, 16))
                    else:
                        k.on(self.eng, lambda e, o=out_ap, i=in_ap, s=sm.h:
                             e.dma_start(o, i).then_inc(s, 16))
                return (sm.h, sm.n)

            def qevents(self):
                return [(sm.h, sm.n) for sm in self.sems if sm.n > 0]

        gq = Q("gpsimd", "qg", 6)
        syq = Q("sync", "qs", 6)
        scq = Q("scalar", "qa", 8)
        y_evs = []

        def t_inc(fn):
            sT.n += 1
            k.on("tensor", lambda e, f=fn, s=sT.h: f(e).then_inc(s, 1))
            return (sT.h, sT.n)

        def v_inc(fn):
            sV.n += 1
            k.on("vector", lambda e, f=fn, s=sV.h: f(e).then_inc(s, 1))
            return (sV.h, sV.n)

        def a_inc(fn):
            sA.n += 1
            k.on("scalar", lambda e, f=fn, s=sA.h: f(e).then_inc(s, 1))
            return (sA.h, sA.n)

        def t_quiesce():
            sT.n += 1
            k.on("tensor", lambda e, s=sT.h: e.drain().then_inc(s, 1))
            return (sT.h, sT.n)

        def mm_group(bank_ap, pairs, extra_wait=()):
            k.waits("tensor", list(extra_wait))
            n = len(pairs)
            for j in range(n - 1):
                l, r = pairs[j]
                k.on("tensor", lambda e, o=bank_ap, ll=l, rr=r, a=(j == 0):
                     e.matmul(o, ll, rr, start=a, stop=False,
                              skip_group_check=True))
            l, r = pairs[-1]
            return t_inc(lambda e, o=bank_ap, ll=l, rr=r, a=(n == 1):
                         e.matmul(o, ll, rr, start=a, stop=True,
                                  skip_group_check=True))

        def ln_stats(src_fn, eps_ap, pre_waits_v, base=0):
            b = base
            k.waits("vector", pre_waits_v)
            k.drain("vector")
            for i in range(4):
                k.on("vector", lambda e, i_=i, b_=b: e.bn_stats(
                    stats[:, b_ + i_ * 6:b_ + (i_ + 1) * 6], src_fn(i_)))
            k.drain("vector")
            ev_st = v_inc(lambda e, b_=b: e.bn_aggr(
                stats[:, b_ + 24:b_ + 26], stats[:, b_:b_ + 24]))
            k.waits("scalar", [ev_st])
            ev_sq = a_inc(lambda e, b_=b: e.activation(
                stats[:, b_ + 29:b_ + 30], stats[:, b_ + 25:b_ + 26],
                AF.Sqrt, bias=eps_ap))
            k.waits("vector", [ev_sq])
            k.on("vector", lambda e, b_=b: e.reciprocal(
                stats[:, b_ + 26:b_ + 27], stats[:, b_ + 29:b_ + 30]))
            k.drain("vector")
            k.on("vector", lambda e, b_=b: e.tensor_tensor(
                stats[:, b_ + 27:b_ + 28], stats[:, b_ + 24:b_ + 25],
                stats[:, b_ + 26:b_ + 27], op=ALU.mult))
            k.drain("vector")
            ev_nmi = v_inc(lambda e, b_=b: e.tensor_scalar_mul(
                stats[:, b_ + 28:b_ + 29], stats[:, b_ + 27:b_ + 28], -1.0))
            return ev_nmi, stats[:, b + 26:b + 27], stats[:, b + 28:b + 29]

        k.on("vector", lambda e: e.memset(stats[:, 88:89], LN_EPS))
        k.on("vector", lambda e: e.memset(stats[:, 89:90], RMS_EPS))
        eps_ln = stats[:, 88:89]
        eps_rms = stats[:, 89:90]
        ev_mask = syq.batch([(maskb[:, :], mask_h[:, :])])
        attnw_hbm = {"wq": wq_h, "wk": wk_h, "wb": wb_h, "wv": wv_h,
                     "wo": wo_h}
        swx_ev = {}
        swx_ev["wq"] = gq.batch([(swx["wq"][:, :], attnw_hbm["wq"][:, :])])
        swx_ev["wk"] = gq.batch([(swx["wk"][:, :], attnw_hbm["wk"][:, :])])
        for nm in ("wb", "wv", "wo"):
            swx_ev[nm] = gq.batch([(swx[nm][:, :], attnw_hbm[nm][:, :])])

        strips = []
        for w_hh, s_hh, nrow in ((wgt_h, swgT, D), (wut_h, swuT, D),
                                 (wdt_h, swdT, FH)):
            nr = nrow // 128
            nc_ = (FH if nrow == D else D) // 2048
            for r in range(nr):
                for c in range(nc_):
                    strips.append(
                        (w_hh[r * 128:(r + 1) * 128, c * 2048:(c + 1) * 2048],
                         s_hh[r * 128:(r + 1) * 128, c * 2048:(c + 1) * 2048]))
        N_STRIPS = len(strips)
        pp = {"i": 0, "sign_ev": [None, None], "out_ev": [None, None]}
        pp_outs = [None] * N_STRIPS
        pp_off = [PP0, PP1]

        def pump(n):
            for _ in range(n):
                i = pp["i"]
                if i >= N_STRIPS:
                    return
                pp["i"] = i + 1
                par = i % 2
                off = pp_off[par]
                src, dst = strips[i]
                k.waits("gpsimd", [pp["out_ev"][par]])
                ev_in = gq.batch([(ph3b[:, off:off + 2048], src)])
                k.waits("scalar", [ev_in])
                ev_sg = a_inc(lambda e, o=off: e.sign(
                    ph3b[:, o:o + 2048], ph3b[:, o:o + 2048]))
                pp["sign_ev"][par] = ev_sg
                k.waits("scalar", [ev_sg])
                pp["out_ev"][par] = scq.batch([(dst,
                                                 ph3b[:, off:off + 2048])])
                pp_outs[i] = pp["out_ev"][par]

        wh_consumed = [None, None]

        def load_half(par, nm, half):
            k.waits("scalar", [wh_consumed[par], swx_ev[nm]])
            items = []
            for gblk in range(4):
                o3 = ph2[par][:, gblk * 4096:(gblk + 1) * 4096].rearrange(
                    "p (a c) -> p a c", a=4)
                i3 = swx[nm][gblk * 512:(gblk + 1) * 512,
                             half * 1024:(half + 1) * 1024]
                i3t = i3.rearrange("(a q) c -> a q c", a=4).transpose([1, 0, 2])
                items.append((o3, i3t))
            return scq.batch(items)

        HW = ["wq", "wk", "wb", "wv", "wo"]
        half_sched = [(t, widx, half) for t in range(NTA)
                      for widx in range(5) for half in (0, 1)]
        half_evs, half_par = {}, {}
        hs = {"i": 0}

        def issue_halves(n):
            for _ in range(n):
                i = hs["i"]
                if i >= len(half_sched):
                    return
                hs["i"] = i + 1
                t, widx, half = half_sched[i]
                par = i % 2
                half_evs[(t, widx, half)] = load_half(par, HW[widx], half)
                half_par[(t, widx, half)] = par

        rot_free = [None] * 8
        stage_ev = [None] * 4
        beta_ev = {}
        att_sb_free = [None, None]
        attbank_free = [None, None]
        obank_free = [None, None]
        state_evac_ev = [None] * 4
        state_mm_ev = [None] * 4
        lnin_free = None
        outst_free = None
        ln1_free = [None, None]
        xfm_free = None
        qfm_free = None
        kfm_free = None
        ktok_ufree = None
        prev_qtrr = None
        prev_ktrr = None
        stream_done = [None] * NTA
        ev_q_trr = [None] * NH
        ev_k_trr = [None] * NH

        issue_halves(2)

        def proj_tm(t, widx, evac):
            for half in (0, 1):
                par = half_par[(t, widx, half)]
                lev = half_evs[(t, widx, half)]
                for fgl in range(2):
                    fg = half * 2 + fgl
                    for tb in range(NTB):
                        bi = 4 + (tb % 2) * 2 + fgl
                        pairs = []
                        for dc in range(NDC):
                            lhsT = x_fm[:, dc * 512 + tb * 128:
                                        dc * 512 + (tb + 1) * 128]
                            rhs = ph2[par][:, dc * 1024 + fgl * 512:
                                           dc * 1024 + (fgl + 1) * 512]
                            pairs.append((lhsT, rhs))
                        ev = mm_group(pb[bi][:, :], pairs,
                                      extra_wait=[lev, rot_free[bi]])
                        rot_free[bi] = evac(tb, fg, bi, ev)
                wh_consumed[par] = t_quiesce()
                issue_halves(1)

        def emit_quant_tb(f, tb, use_wdb=False):
            frow = f * TF
            geb = None if use_wdb else ph3a[:, 0:2048]
            k.waits("gpsimd", stream_done[f] + [self_outst_free[0]])
            ld_ev = gq.batch([(outst[:, :],
                               sstream[frow + tb * 128:
                                       frow + (tb + 1) * 128, :])])
            ev_nmi, inv_ap, nmi_ap = ln_stats(
                lambda i: outst[:, i * 512:(i + 1) * 512],
                eps_rms, [ld_ev, self_lnin_free[0], self_tmpf_free[0],
                          self_geb_gate[0]], base=0)
            k.waits("scalar", [ev_nmi, self_lnin_free[0]])
            ev_rm = a_inc(lambda e, sc_=inv_ap, bi_=nmi_ap: e.activation(
                lnin[:, :], outst[:, :], AF.Identity, bias=bi_, scale=sc_))
            k.waits("vector", [ev_rm])
            k.on("vector", lambda e: e.tensor_reduce(
                stats[:, 80:81], lnin[:, :], axis=AX.X, op=ALU.max,
                apply_absolute_value=True))
            k.drain("vector")
            k.on("vector", lambda e: e.reciprocal(stats[:, 81:82],
                                                  stats[:, 80:81]))
            k.drain("vector")
            k.on("vector", lambda e: e.tensor_scalar_mul(
                stats[:, 82:83], stats[:, 81:82], 127.0))
            k.on("vector", lambda e: e.tensor_scalar_mul(
                stats[:, 83:84], stats[:, 80:81], 1.0 / 127.0))
            k.drain("vector")
            ev_qs = v_inc(lambda e: e.tensor_scalar_mul(
                outst[:, :], lnin[:, :], stats[:, 82:83]))
            k.waits("scalar", [ev_qs])
            k.on("scalar", lambda e: e.sign(lnin[:, :], outst[:, :]))
            ev_ab = a_inc(lambda e: e.activation(tmpf[:, :], outst[:, :],
                                                 AF.Abs))
            k.waits("vector", [ev_ab])
            k.on("vector", lambda e: e.tensor_scalar_add(
                tmpf[:, :], tmpf[:, :], 0.5))
            k.drain("vector")
            k.on("vector", lambda e: e.tensor_copy(outst[:, :], tmpf[:, :]))
            k.drain("vector")
            for kbit in (64.0, 32.0, 16.0, 8.0, 4.0, 2.0, 1.0):
                if use_wdb:
                    for j in range(4):
                        k.on("vector", lambda e, kb=kbit, j_=j:
                             e.tensor_scalar(
                                 wdb[j_][:, :],
                                 tmpf[:, j_ * 512:(j_ + 1) * 512], kb, kb,
                                 op0=ALU.is_ge, op1=ALU.mult))
                    k.drain("vector")
                    for j in range(4):
                        k.on("vector", lambda e, j_=j: e.tensor_tensor(
                            tmpf[:, j_ * 512:(j_ + 1) * 512],
                            tmpf[:, j_ * 512:(j_ + 1) * 512],
                            wdb[j_][:, :], op=ALU.subtract))
                    k.drain("vector")
                else:
                    k.on("vector", lambda e, kb=kbit: e.tensor_scalar(
                        geb, tmpf[:, :], kb, kb, op0=ALU.is_ge, op1=ALU.mult))
                    k.drain("vector")
                    k.on("vector", lambda e: e.tensor_tensor(
                        tmpf[:, :], tmpf[:, :], geb, op=ALU.subtract))
                    k.drain("vector")
            k.on("vector", lambda e: e.tensor_tensor(
                tmpf[:, :], outst[:, :], tmpf[:, :], op=ALU.subtract))
            k.drain("vector")
            k.on("vector", lambda e: e.tensor_scalar_min(
                tmpf[:, :], tmpf[:, :], 127.0))
            k.drain("vector")
            ev_xq = v_inc(lambda e: e.scalar_tensor_tensor(
                outst[:, :], lnin[:, :], stats[:, 83:84], tmpf[:, :],
                op0=ALU.mult, op1=ALU.mult))
            self_lnin_free[0] = ev_xq
            self_tmpf_free[0] = ev_xq
            self_outst_free[0] = ev_xq
            k.waits("gpsimd", [ev_xq])
            wr = gq.batch([(sxq[frow + tb * 128:frow + (tb + 1) * 128, :],
                            outst[:, :])])
            self_outst_free[0] = wr
            return wr, (sV.h, sV.n)

        def emit_xq_trr(f, xq_wr, extra):
            frow = f * TF
            k.waits("sync", xq_wr + extra)
            return syq.batch(
                [(xq_fm[:, dc * 512:(dc + 1) * 512],
                  sxq[frow:frow + TF, dc * 128:(dc + 1) * 128])
                 for dc in range(NDC)], transpose=True)

        self_outst_free = [None]
        self_lnin_free = [None]
        self_tmpf_free = [None]
        self_geb_gate = [None]

        hoisted_xfm = None
        hq_wr = []
        hq_done = None
        for t in range(NTA):
            trow = t * TA

            if hoisted_xfm is not None:
                ev_xfm = hoisted_xfm
            else:
                k.waits("gpsimd", [xfm_free])
                ev_xfm = gq.batch(
                    [(x_fm[:, dc * 512:(dc + 1) * 512],
                      xt_h[dc * 128:(dc + 1) * 128, trow:trow + TA])
                     for dc in range(NDC)])

            def q_evac(tb, fg, bi, mm_ev):
                j = (tb * 4 + fg) % 4
                sl = ph3b[:, SS + j * 512:SS + (j + 1) * 512]
                k.waits("vector", [mm_ev, stage_ev[j]])
                v_ev = v_inc(lambda e, o=sl, i=pb[bi][:, :]:
                             e.tensor_copy(o, i))
                k.waits("scalar", [v_ev, prev_qtrr])
                stage_ev[j] = scq.batch(
                    [(sq[tb * 128:(tb + 1) * 128,
                         fg * 512:(fg + 1) * 512], sl)])
                return v_ev

            k.waits("tensor", [ev_xfm])
            proj_tm(t, 0, q_evac)
            if t >= 2:
                self_lnin_free[0] = ln1_free[0]
                self_tmpf_free[0] = ln1_free[1]
                for tb_h in ((0, 1) if t == 2 else (2, 3)):
                    wr, hq_done = emit_quant_tb(0, tb_h, use_wdb=True)
                    hq_wr.append(wr)
                ln1_free[0] = hq_done
                ln1_free[1] = hq_done
            q_spills = list(stage_ev)
            pump(10)

            k.waits("sync", q_spills + [qfm_free])
            ev_qtrr_all = syq.batch(
                [(q_fm[:, h * 512:(h + 1) * 512], sq[:, h * 128:(h + 1) * 128])
                 for h in range(NH)], transpose=True)
            prev_qtrr = ev_qtrr_all

            def k_evac(tb, fg, bi, mm_ev):
                sl = k_tok[:, tb * 2048 + fg * 512:tb * 2048 + (fg + 1) * 512]
                k.waits("vector", [mm_ev, ktok_ufree])
                v_ev = v_inc(lambda e, o=sl, i=pb[bi][:, :]:
                             e.tensor_copy(o, i))
                k.waits("scalar", [v_ev, prev_ktrr])
                k_spill_evs.append(scq.batch(
                    [(sk[tb * 128:(tb + 1) * 128,
                         fg * 512:(fg + 1) * 512], sl)]))
                return v_ev

            k_spill_evs = []
            proj_tm(t, 1, k_evac)
            pump(10)

            k.waits("sync", k_spill_evs + [kfm_free])
            ev_ktrr_all = syq.batch(
                [(k_fm[:, h * 512:(h + 1) * 512], sk[:, h * 128:(h + 1) * 128])
                 for h in range(NH)], transpose=True)
            prev_ktrr = ev_ktrr_all

            def b_evac(tb, fg, bi, mm_ev):
                sl = u_tok[:, tb * 2048 + fg * 512:tb * 2048 + (fg + 1) * 512]
                k.waits("scalar", [mm_ev, ktok_ufree])
                ev = a_inc(lambda e, o=sl, i=pb[bi][:, :]:
                           e.activation(o, i, AF.Sigmoid))
                beta_ev[(tb, fg)] = ev
                return ev

            proj_tm(t, 2, b_evac)
            pump(10)

            def v_evac(tb, fg, bi, mm_ev):
                sl = u_tok[:, tb * 2048 + fg * 512:tb * 2048 + (fg + 1) * 512]
                k.waits("vector", [mm_ev, beta_ev[(tb, fg)]])
                return v_inc(lambda e, o=sl, i=pb[bi][:, :]:
                             e.tensor_tensor(o, o, i, op=ALU.mult))

            proj_tm(t, 3, v_evac)
            u_done = (sV.h, sV.n)
            xfm_free = t_quiesce()
            pump(10)
            if t + 1 < NTA:
                k.waits("gpsimd", [xfm_free])
                hoisted_xfm = gq.batch(
                    [(x_fm[:, dc * 512:(dc + 1) * 512],
                      xt_h[dc * 128:(dc + 1) * 128,
                           (t + 1) * TA:(t + 2) * TA])
                     for dc in range(NDC)])

            for c in range(CPT):
                g = t * CPT + c
                boff = (c % 2) * 64
                tb = c // 2
                mask_evs = [None, None]
                for hb in range(2):
                    bi = 4 + hb
                    k.waits("tensor", [attbank_free[hb], rot_free[4 + hb],
                                       ev_ktrr_all, ev_qtrr_all])
                    p1_ev = None
                    for hh in range(8):
                        h = hb * 8 + hh
                        l = k_fm[:, h * 512 + c * 64:h * 512 + (c + 1) * 64]
                        r = q_fm[:, h * 512 + c * 64:h * 512 + (c + 1) * 64]
                        o = pb[bi][boff:boff + 64, hh * 64:(hh + 1) * 64]
                        p1_ev = t_inc(lambda e, oo=o, ll=l, rr=r: e.matmul(
                            oo, ll, rr, start=True, stop=True,
                            skip_group_check=True))
                    ato = AT0 if hb == 0 else AT1
                    k.waits("vector", [p1_ev, att_sb_free[hb], ev_mask])
                    mask_evs[hb] = v_inc(
                        lambda e, o=ph3b[boff:boff + 64, ato:ato + 512],
                        i=pb[bi][boff:boff + 64, :],
                        m=maskb[boff:boff + 64, :]:
                        e.tensor_tensor(o, i, m, op=ALU.mult))
                    attbank_free[hb] = mask_evs[hb]
                for hb in range(2):
                    obi = 6 + hb
                    ato = AT0 if hb == 0 else AT1
                    k.waits("tensor", [mask_evs[hb], obank_free[hb],
                                       rot_free[6 + hb], u_done])
                    last_o = None
                    for hh in range(8):
                        h = hb * 8 + hh
                        sbk = h // 4
                        u_l = u_tok[boff:boff + 64,
                                    tb * 2048 + h * 128:
                                    tb * 2048 + (h + 1) * 128]
                        at_r = ph3b[boff:boff + 64,
                                    ato + hh * 64:ato + (hh + 1) * 64]
                        o_ap = pb[obi][:, hh * 64:(hh + 1) * 64]
                        k.waits("tensor", [state_evac_ev[sbk]])
                        if g > 0:
                            k.on("tensor", lambda e, oo=o_ap, ll=u_l, rr=at_r:
                                 e.matmul(oo, ll, rr, start=True, stop=False,
                                          skip_group_check=True))
                            st_l = state_sb[:, h * 128:(h + 1) * 128]
                            q_r = q_fm[:, h * 512 + c * 64:
                                       h * 512 + (c + 1) * 64]
                            last_o = t_inc(
                                lambda e, oo=o_ap, ll=st_l, rr=q_r: e.matmul(
                                    oo, ll, rr, start=False, stop=True,
                                    skip_group_check=True))
                        else:
                            last_o = t_inc(
                                lambda e, oo=o_ap, ll=u_l, rr=at_r: e.matmul(
                                    oo, ll, rr, start=True, stop=True,
                                    skip_group_check=True))
                        kt_l = k_tok[boff:boff + 64,
                                     tb * 2048 + h * 128:
                                     tb * 2048 + (h + 1) * 128]
                        st_o = pb[sbk][:, (h % 4) * 128:(h % 4 + 1) * 128]
                        a0 = (g == 0 and h % 4 == 0)
                        b0 = (g == NCHUNK - 1 and h % 4 == 3)
                        if h % 4 == 3:
                            state_mm_ev[sbk] = t_inc(
                                lambda e, oo=st_o, ll=kt_l, rr=u_l, a=a0,
                                b=b0: e.matmul(oo, ll, rr, start=a, stop=b,
                                               skip_group_check=True))
                        else:
                            k.on("tensor",
                                 lambda e, oo=st_o, ll=kt_l, rr=u_l, a=a0,
                                 b=b0: e.matmul(oo, ll, rr, start=a, stop=b,
                                                skip_group_check=True))
                    k.waits("vector", [last_o])
                    oq = q_fm[:, :].rearrange("p (h s) -> p h s", h=NH)
                    opv = pb[obi][:, :].rearrange("p (h s) -> p h s", h=8)
                    oev = v_inc(lambda e,
                                o=oq[:, hb * 8:(hb + 1) * 8,
                                     c * 64:(c + 1) * 64],
                                i=opv: e.tensor_copy(o, i))
                    obank_free[hb] = oev
                chunk_q = t_quiesce()
                att_sb_free[0] = chunk_q
                att_sb_free[1] = chunk_q
                if g < NCHUNK - 1:
                    for sbk in range(4):
                        k.waits("vector", [state_mm_ev[sbk]])
                        state_evac_ev[sbk] = v_inc(
                            lambda e,
                            o=state_sb[:, sbk * 512:(sbk + 1) * 512],
                            i=pb[sbk][:, :]: e.tensor_copy(o, i))
            scan_o_done = (sV.h, sV.n)
            scan_mm_done = chunk_q
            ktok_ufree = scan_mm_done
            kfm_free = scan_mm_done
            pump(8)

            stream_evs = []
            xld = [None] * NTB
            for tb in (0, 1):
                par2 = tb % 2
                buf = lnin if par2 == 0 else tmpf
                k.waits("scalar", [ln1_free[par2]])
                xld[tb] = scq.batch(
                    [(buf[:, :],
                      x_h[trow + tb * 128:trow + (tb + 1) * 128, :])])
            for tb in range(NTB):
                par2 = tb % 2
                buf = lnin if par2 == 0 else tmpf
                if xld[tb] is None:
                    k.waits("scalar", [ln1_free[par2]])
                    xld[tb] = scq.batch(
                        [(buf[:, :],
                          x_h[trow + tb * 128:trow + (tb + 1) * 128, :])])
                evac_evs = []
                for dg in range(4):
                    half = dg // 2
                    dgl = dg % 2
                    par = half_par[(t, 4, half)]
                    lev = half_evs[(t, 4, half)]
                    bi = 4 + (tb % 2) * 2 + dgl
                    pairs = []
                    for fc in range(NDC):
                        lhsT = q_fm[:, fc * 512 + tb * 128:
                                    fc * 512 + (tb + 1) * 128]
                        rhs = ph2[par][:, fc * 1024 + dgl * 512:
                                       fc * 1024 + (dgl + 1) * 512]
                        pairs.append((lhsT, rhs))
                    ev = mm_group(pb[bi][:, :], pairs,
                                  extra_wait=[lev, rot_free[bi], scan_o_done])
                    k.waits("vector", [ev, xld[tb]])
                    eev = v_inc(lambda e, o=buf[:, dg * 512:(dg + 1) * 512],
                                i=pb[bi][:, :]: e.tensor_tensor(
                                    o, o, i, op=ALU.add))
                    rot_free[bi] = eev
                    evac_evs.append(eev)
                ev_nmi, inv_ap, nmi_ap = ln_stats(
                    lambda i, b_=buf: b_[:, i * 512:(i + 1) * 512], eps_ln,
                    evac_evs, base=36 * par2)
                k.waits("scalar", [ev_nmi])
                ap_ev = a_inc(lambda e, b_=buf, sc_=inv_ap,
                              bi_=nmi_ap: e.activation(
                                  b_[:, :], b_[:, :], AF.Identity, bias=bi_,
                                  scale=sc_))
                k.waits("gpsimd", [ap_ev])
                ln1_free[par2] = gq.batch(
                    [(sstream[trow + tb * 128:trow + (tb + 1) * 128, :],
                      buf[:, :])])
                stream_evs.append(ln1_free[par2])
            stream_done[t] = stream_evs
            wo_q = t_quiesce()
            qfm_free = wo_q
            wh_consumed[half_par[(t, 4, 0)]] = wo_q
            wh_consumed[half_par[(t, 4, 1)]] = wo_q
            issue_halves(2)
            pump(8)

        pump(N_STRIPS)
        tdone_list = scq.qevents()
        ppd_list = tdone_list
        attn_mm = wo_q
        attn_v = (sV.h, sV.n)
        attn_a = (sA.h, sA.n)

        p_sl = lambda fc: ph2[fc // 32][:, (fc % 32) * 512:(fc % 32 + 1) * 512]
        stream_sl = lambda tb: ph3b[:, tb * 2048:(tb + 1) * 2048]

        span_free = [None, None]
        pref_spans = {}
        dbg_evs = []
        quant_init = False
        wdb_free = [None] * 4
        y_ev = None
        prev_ln2_use = None
        prev_xq_trr = None
        prev_xq_wr = None
        prev_u_done = None
        prev_wd_read = None
        ln2_usage = None

        self_lnin_free[0] = ln1_free[0]
        self_tmpf_free[0] = ln1_free[1]
        for f in range(NTF):
            frow = f * TF

            if f == 0:
                self_geb_gate[0] = attn_mm
                ev_xq_trr = emit_xq_trr(0, hq_wr, [attn_mm, hq_done])

            stream_ld = [None] * NTB
            for tb in range(NTB):
                k.waits("gpsimd", stream_done[f] +
                        [prev_wd_read, prev_ln2_use] +
                        (tdone_list + [attn_v, attn_mm, attn_a]
                         if f == 0 else []))
                stream_ld[tb] = gq.batch(
                    [(stream_sl(tb),
                      sstream[frow + tb * 128:frow + (tb + 1) * 128, :])])

            def load_span(par, w_sT, sidx, first_pair, phase_extra=None):
                extra = []
                if f == 0 and first_pair:
                    extra.extend(tdone_list)
                if first_pair:
                    extra.append(prev_u_done)
                    if phase_extra is not None:
                        extra.append(phase_extra)
                a, b = ph1[par * 2], ph1[par * 2 + 1]
                k.waits("sync", [span_free[par]] + extra)
                k.waits("scalar", [span_free[par]] + extra)
                it_s, it_a = [], []
                for dc in range(NDC):
                    dst = a if dc < 8 else b
                    dsl = dst[:, (dc % 8) * 1024:(dc % 8 + 1) * 1024]
                    src = w_sT[dc * 128:(dc + 1) * 128,
                               sidx * 1024:(sidx + 1) * 1024]
                    (it_s if dc % 2 == 0 else it_a).append((dsl, src))
                evs = syq.batch(it_s)
                evsc = scq.batch(it_a)
                return (evs, evsc)

            gu_rot = [None, None, None]

            def gu_phase(w_s, is_g, phase_extra=None):
                for sidx in range(8):
                    par = sidx % 2
                    if is_g and sidx < 2 and pref_spans.get(sidx) is not None:
                        lev = pref_spans.pop(sidx)
                    else:
                        lev = load_span(par, w_s, sidx, sidx < 2, phase_extra)
                    a, b = ph1[par * 2], ph1[par * 2 + 1]
                    last = None
                    for jfc in range(8):
                        fc = sidx * 8 + jfc
                        bi = jfc % 3
                        pairs = []
                        for dc in range(NDC):
                            src = a if dc < 8 else b
                            lhsT = src[:, (dc % 8) * 1024 + jfc * 128:
                                       (dc % 8) * 1024 + (jfc + 1) * 128]
                            rhs = xq_fm[:, dc * 512:(dc + 1) * 512]
                            pairs.append((lhsT, rhs))
                        ev = mm_group(pb[bi][:, :], pairs,
                                      extra_wait=[lev[0], lev[1], gu_rot[bi],
                                                  ev_xq_trr])
                        if is_g:
                            k.waits("scalar", [ev])
                            gu_rot[bi] = a_inc(
                                lambda e, o=p_sl(fc), i=pb[bi][:, :]:
                                e.activation(o, i, AF.Silu))
                        else:
                            k.waits("vector", [ev, gu_rot[bi]])
                            gu_rot[bi] = v_inc(
                                lambda e, o=p_sl(fc), i=pb[bi][:, :]:
                                e.tensor_tensor(o, o, i, op=ALU.mult))
                        last = ev
                    span_free[par] = t_quiesce()

            gu_phase(swgT, True)
            g_done_a = (sA.h, sA.n)
            gu_phase(swuT, False, phase_extra=g_done_a)
            u_done_v = (sV.h, sV.n)
            u_mm_done = t_quiesce()
            prev_u_done = u_mm_done
            if f + 1 < NTF:
                self_geb_gate[0] = u_mm_done
                next_xq_wr = []
            if dbg and f == NTF - 1:
                k.waits("scalar", [u_done_v, ev_xq_trr])
                dbg_evs.append(scq.batch([(dbg_p[:, 0:16384], ph2[0][:, :])]))
                dbg_evs.append(scq.batch([(dbg_p[:, 16384:32768],
                                           ph2[1][:, :])]))
                dbg_evs.append(scq.batch([(dbg_xqf[:, :], ph3a[:, :])]))

            for dg in range(4):
                if f + 1 < NTF:
                    wr, qd = emit_quant_tb(f + 1, dg)
                    next_xq_wr.append(wr)
                    next_quant_done = qd
                for fc in range(NFC):
                    par4 = fc % 4
                    qeng = "sync" if fc % 2 == 0 else "scalar"
                    qq = syq if fc % 2 == 0 else scq
                    k.waits(qeng, [wdb_free[par4]])
                    lev = qq.batch(
                        [(wdb[par4][:, :],
                          swdT[fc * 128:(fc + 1) * 128,
                               dg * 512:(dg + 1) * 512])])
                    wm_extra = [lev, u_done_v, g_done_a] if fc == 0 else [lev]
                    k.waits("tensor", wm_extra)
                    for tb in range(NTB):
                        bi = 4 + tb
                        if fc == 0:
                            k.waits("tensor", [rot_free[bi]])
                        o = pb[bi][:, :]
                        l = p_sl(fc)[:, tb * 128:(tb + 1) * 128]
                        r = wdb[par4][:, :]
                        ev = t_inc(lambda e, oo=o, ll=l, rr=r,
                                   a=(fc == 0), b=(fc == NFC - 1):
                                   e.matmul(oo, ll, rr, start=a, stop=b,
                                            skip_group_check=True))
                        if tb == NTB - 1:
                            wdb_free[par4] = ev
                wd_done = (sT.h, sT.n)
                for tb in range(NTB):
                    bi = 4 + tb
                    k.waits("vector", [wd_done, u_mm_done, u_done_v,
                                       stream_ld[tb]])
                    ev = v_inc(lambda e,
                               o=stream_sl(tb)[:, dg * 512:(dg + 1) * 512],
                               i=pb[bi][:, :]:
                               e.tensor_tensor(o, o, i, op=ALU.add))
                    rot_free[bi] = ev
            ln2_ready = (sV.h, sV.n)
            prev_wd_read = ln2_ready
            if f + 1 < NTF:
                next_xq_trr = emit_xq_trr(f + 1, next_xq_wr,
                                          [next_quant_done])
                for sidx in (0, 1):
                    pref_spans[sidx] = load_span(sidx % 2, swgT, sidx, True)

            for tb in range(NTB):
                par2 = tb % 2
                buf = outst if par2 == 0 else tmpf
                buf_free = self_outst_free if par2 == 0 else self_tmpf_free
                ev_nmi, inv_ap, nmi_ap = ln_stats(
                    lambda i, tb_=tb: stream_sl(tb_)[:, i * 512:(i + 1) * 512],
                    eps_ln, [ln2_ready, buf_free[0]], base=36 * par2)
                k.waits("scalar", [ev_nmi, buf_free[0]])
                ap_ev = a_inc(lambda e, tb_=tb, o=buf, sc_=inv_ap,
                              bi_=nmi_ap: e.activation(
                                  o[:, :], stream_sl(tb_), AF.Identity,
                                  bias=bi_, scale=sc_))
                ln2_usage = ap_ev
                prev_ln2_use = ap_ev
                buf_free[0] = ap_ev
                k.waits("scalar", [ap_ev])
                y_ev = scq.batch(
                    [(y_h[frow + tb * 128:frow + (tb + 1) * 128, :],
                      buf[:, :])])
                y_evs.append(y_ev)
                buf_free[0] = y_ev
            if f + 1 < NTF:
                ev_xq_trr = next_xq_trr

        k.waits("sync", y_evs + dbg_evs)

        with nc.Block() as block:
            @block.sync
            def _(e):
                for fn in k.items["sync"]:
                    fn(e)

            @block.scalar
            def _(e):
                for fn in k.items["scalar"]:
                    fn(e)

            @block.tensor
            def _(e):
                for fn in k.items["tensor"]:
                    fn(e)

            @block.vector
            def _(e):
                for fn in k.items["vector"]:
                    fn(e)

            @block.gpsimd
            def _(e):
                for fn in k.items["gpsimd"]:
                    fn(e)

    return nc


_MASK = None


def _mask_const():
    global _MASK
    if _MASK is None:
        tri = np.triu(np.ones((64, 64), dtype=np.float32))
        _MASK = np.ascontiguousarray(np.tile(tri, (2, 8)))
    return _MASK


def kernel(L0_attn_hidden_states, L0_attn_Wq, L0_attn_Wk, L0_attn_Wv,
           L0_attn_Wb, L0_attn_Wo, L0_ffn_W_g, L0_ffn_W_u, L0_ffn_W_d):
    x = np.ascontiguousarray(np.asarray(L0_attn_hidden_states,
                                        dtype=np.float32))
    ws = {
        "wq": np.ascontiguousarray(np.asarray(L0_attn_Wq, dtype=np.float32)),
        "wk": np.ascontiguousarray(np.asarray(L0_attn_Wk, dtype=np.float32)),
        "wv": np.ascontiguousarray(np.asarray(L0_attn_Wv, dtype=np.float32)),
        "wb": np.ascontiguousarray(np.asarray(L0_attn_Wb, dtype=np.float32)),
        "wo": np.ascontiguousarray(np.asarray(L0_attn_Wo, dtype=np.float32)),
        "wgt": np.ascontiguousarray(np.asarray(L0_ffn_W_g, dtype=np.float32).T),
        "wut": np.ascontiguousarray(np.asarray(L0_ffn_W_u, dtype=np.float32).T),
        "wdt": np.ascontiguousarray(np.asarray(L0_ffn_W_d, dtype=np.float32).T),
        "maskb": _mask_const(),
    }
    nc = build()
    in_maps = []
    for b in range(8):
        m = {"x": np.ascontiguousarray(x[b]),
             "xt": np.ascontiguousarray(x[b].T)}
        m.update(ws)
        in_maps.append(m)
    res = run_bass_kernel_spmd(nc, in_maps, core_ids=list(range(8)))
    return np.stack([r["y"] for r in res.results], axis=0)


# version 19 (final)
# speedup vs baseline: 1.0513x; 1.0156x over previous
import contextlib
import numpy as np
import concourse.bass as bass
import concourse.mybir as mybir
from concourse.bass_utils import run_bass_kernel_spmd

F32 = mybir.dt.float32
BF16 = mybir.dt.bfloat16
I32 = mybir.dt.int32
AF = mybir.ActivationFunctionType
ALU = mybir.AluOpType
AX = mybir.AxisListType

S = 2048
D = 2048
NH = 16
FH = 8192
CH = 64
TA = 512
TF = 512
NTA = S // TA
NTF = S // TF
NTB = TA // 128
NDC = D // 128
NFC = FH // 128
CPT = TA // CH
NCHUNK = S // CH
LN_EPS = 1e-5
RMS_EPS = 1e-8

ENGS = ("sync", "scalar", "tensor", "vector", "gpsimd")


class Sem:
    def __init__(self, stack, nc, name):
        self.h = stack.enter_context(nc.semaphore(name))
        self.n = 0


class K:
    def __init__(self):
        self.items = {e: [] for e in ENGS}

    def on(self, eng, fn):
        self.items[eng].append(fn)

    def waits(self, eng, evs):
        best = {}
        for ev in evs:
            if ev is None:
                continue
            sem, val = ev
            key = id(sem)
            if key not in best or val > best[key][1]:
                best[key] = (sem, val)
        for sem, val in best.values():
            if val > 0:
                self.on(eng, lambda e, s=sem, v=val: e.wait_ge(s, v))

    def drain(self, eng):
        self.on(eng, lambda e: e.drain())


def build(dbg=False):
    nc = bass.Bass("TRN2", target_bir_lowering=False)
    DK = "ExternalOutput" if dbg else "Internal"  

    x_h = nc.dram_tensor("x", [S, D], F32, kind="ExternalInput").ap()
    wq_h = nc.dram_tensor("wq", [D, D], F32, kind="ExternalInput").ap()
    wk_h = nc.dram_tensor("wk", [D, D], F32, kind="ExternalInput").ap()
    wb_h = nc.dram_tensor("wb", [D, D], F32, kind="ExternalInput").ap()
    wv_h = nc.dram_tensor("wv", [D, D], F32, kind="ExternalInput").ap()
    wo_h = nc.dram_tensor("wo", [D, D], F32, kind="ExternalInput").ap()
    wgt_h = nc.dram_tensor("wgt", [D, FH], F32, kind="ExternalInput").ap()
    wut_h = nc.dram_tensor("wut", [D, FH], F32, kind="ExternalInput").ap()
    wdt_h = nc.dram_tensor("wdt", [FH, D], F32, kind="ExternalInput").ap()
    xt_h = nc.dram_tensor("xt", [D, S], F32, kind="ExternalInput").ap()
    mask_h = nc.dram_tensor("maskb", [128, 512], F32, kind="ExternalInput").ap()
    y_h = nc.dram_tensor("y", [S, D], F32, kind="ExternalOutput").ap()

    swx = {nm: nc.dram_tensor("s" + nm, [D, D], BF16, kind="Internal").ap()
           for nm in ("wq", "wk", "wb", "wv", "wo")}
    swgT = nc.dram_tensor("swgT", [D, FH], BF16, kind="Internal").ap()
    swuT = nc.dram_tensor("swuT", [D, FH], BF16, kind="Internal").ap()
    swdT = nc.dram_tensor("swdT", [FH, D], BF16, kind="Internal").ap()
    sq = nc.dram_tensor("sq", [TA, D], BF16, kind=DK).ap()
    sk = nc.dram_tensor("sk", [TA, D], BF16, kind=DK).ap()
    sstream = nc.dram_tensor("sstream", [S, D], BF16, kind=DK).ap()
    sxq = nc.dram_tensor("sxq", [S, D], BF16, kind=DK).ap()
    dbg_p = nc.dram_tensor("dbg_p", [128, 32768], BF16, kind=DK).ap() if dbg else None
    dbg_xqf = nc.dram_tensor("dbg_xqf", [128, 8192], BF16, kind=DK).ap() if dbg else None

    stack = contextlib.ExitStack()
    with stack:
        sb = lambda name, shape, dt, **kw: stack.enter_context(
            nc.sbuf_tensor(name, shape, dt, **kw))
        psb = lambda name: stack.enter_context(
            nc.psum_tensor(name, [128, 512], F32))

        ph1 = [sb(f"ph1_{i}", [128, 8192], BF16, is_transpose_dest=True)
               for i in range(4)]
        ph2 = [sb(f"ph2_{i}", [128, 16384], BF16) for i in range(2)]
        ph3a = sb("ph3a", [128, 8192], BF16, is_transpose_dest=True)
        ph3b = sb("ph3b", [128, 8192], BF16)
        state_sb = sb("state_sb", [128, NH * 128], BF16)
        lnin = sb("lnin", [128, D], F32)
        outst = sb("outst", [128, D], F32)
        tmpf = sb("tmpf", [128, D], F32)
        maskb = sb("mask_sb", [128, 512], F32)
        stats = sb("stats", [128, 96], F32)
        wdb = [sb(f"wdb_{i}", [128, 512], BF16, is_transpose_dest=True)
               for i in range(4)]


        x_fm, q_fm, k_fm, k_tok = ph1
        u_tok = ph3a
        xq_fm = ph3a
        PP0, PP1, SS = 0, 2048, 4096
        AT0, AT1 = 6144, 6656
        stg = ph3b[:, SS:SS + 2048]

        pb = [psb(f"pb{i}") for i in range(8)]

        sT = Sem(stack, nc, "sT")
        sV = Sem(stack, nc, "sV")
        sA = Sem(stack, nc, "sA")

        k = K()

        class Q:

            def __init__(self, eng, name, nsem):
                self.eng = eng
                self.sems = [Sem(stack, nc, f"{name}{i}") for i in range(nsem)]
                self.i = 0

            def batch(self, items, accum=False, transpose=False):
                sm = self.sems[self.i % len(self.sems)]
                self.i += 1
                k.waits(self.eng, [(sm.h, sm.n)])
                for out_ap, in_ap in items:
                    sm.n += 16
                    if transpose:
                        k.on(self.eng, lambda e, o=out_ap, i=in_ap, s=sm.h:
                             e.dma_start(o, i, transpose=True).then_inc(s, 16))
                    elif accum:
                        k.on(self.eng, lambda e, o=out_ap, i=in_ap, s=sm.h:
                             e.dma_start(o, i, accum_op=ALU.add)
                             .then_inc(s, 16))
                    else:
                        k.on(self.eng, lambda e, o=out_ap, i=in_ap, s=sm.h:
                             e.dma_start(o, i).then_inc(s, 16))
                return (sm.h, sm.n)

            def qevents(self):
                return [(sm.h, sm.n) for sm in self.sems if sm.n > 0]

        gq = Q("gpsimd", "qg", 6)
        syq = Q("sync", "qs", 6)
        scq = Q("scalar", "qa", 8)
        y_evs = []

        def t_inc(fn):
            sT.n += 1
            k.on("tensor", lambda e, f=fn, s=sT.h: f(e).then_inc(s, 1))
            return (sT.h, sT.n)

        def v_inc(fn):
            sV.n += 1
            k.on("vector", lambda e, f=fn, s=sV.h: f(e).then_inc(s, 1))
            return (sV.h, sV.n)

        def a_inc(fn):
            sA.n += 1
            k.on("scalar", lambda e, f=fn, s=sA.h: f(e).then_inc(s, 1))
            return (sA.h, sA.n)

        def t_quiesce():
            sT.n += 1
            k.on("tensor", lambda e, s=sT.h: e.drain().then_inc(s, 1))
            return (sT.h, sT.n)

        def mm_group(bank_ap, pairs, extra_wait=()):
            k.waits("tensor", list(extra_wait))
            n = len(pairs)
            for j in range(n - 1):
                l, r = pairs[j]
                k.on("tensor", lambda e, o=bank_ap, ll=l, rr=r, a=(j == 0):
                     e.matmul(o, ll, rr, start=a, stop=False,
                              skip_group_check=True))
            l, r = pairs[-1]
            return t_inc(lambda e, o=bank_ap, ll=l, rr=r, a=(n == 1):
                         e.matmul(o, ll, rr, start=a, stop=True,
                                  skip_group_check=True))

        def ln_stats(src_fn, eps_ap, pre_waits_v, base=0):
            b = base
            k.waits("vector", pre_waits_v)
            k.drain("vector")
            for i in range(4):
                k.on("vector", lambda e, i_=i, b_=b: e.bn_stats(
                    stats[:, b_ + i_ * 6:b_ + (i_ + 1) * 6], src_fn(i_)))
            k.drain("vector")
            ev_st = v_inc(lambda e, b_=b: e.bn_aggr(
                stats[:, b_ + 24:b_ + 26], stats[:, b_:b_ + 24]))
            k.waits("scalar", [ev_st])
            ev_sq = a_inc(lambda e, b_=b: e.activation(
                stats[:, b_ + 29:b_ + 30], stats[:, b_ + 25:b_ + 26],
                AF.Sqrt, bias=eps_ap))
            k.waits("vector", [ev_sq])
            k.on("vector", lambda e, b_=b: e.reciprocal(
                stats[:, b_ + 26:b_ + 27], stats[:, b_ + 29:b_ + 30]))
            k.drain("vector")
            k.on("vector", lambda e, b_=b: e.tensor_tensor(
                stats[:, b_ + 27:b_ + 28], stats[:, b_ + 24:b_ + 25],
                stats[:, b_ + 26:b_ + 27], op=ALU.mult))
            k.drain("vector")
            ev_nmi = v_inc(lambda e, b_=b: e.tensor_scalar_mul(
                stats[:, b_ + 28:b_ + 29], stats[:, b_ + 27:b_ + 28], -1.0))
            return ev_nmi, stats[:, b + 26:b + 27], stats[:, b + 28:b + 29]

        k.on("vector", lambda e: e.memset(stats[:, 88:89], LN_EPS))
        k.on("vector", lambda e: e.memset(stats[:, 89:90], RMS_EPS))
        eps_ln = stats[:, 88:89]
        eps_rms = stats[:, 89:90]
        ev_mask = syq.batch([(maskb[:, :], mask_h[:, :])])
        attnw_hbm = {"wq": wq_h, "wk": wk_h, "wb": wb_h, "wv": wv_h,
                     "wo": wo_h}
        swx_ev = {}
        swx_ev["wq"] = gq.batch([(swx["wq"][:, :], attnw_hbm["wq"][:, :])])
        swx_ev["wk"] = gq.batch([(swx["wk"][:, :], attnw_hbm["wk"][:, :])])
        for nm in ("wb", "wv", "wo"):
            swx_ev[nm] = gq.batch([(swx[nm][:, :], attnw_hbm[nm][:, :])])

        strips = []
        for w_hh, s_hh, nrow in ((wgt_h, swgT, D), (wut_h, swuT, D),
                                 (wdt_h, swdT, FH)):
            nr = nrow // 128
            nc_ = (FH if nrow == D else D) // 2048
            for r in range(nr):
                for c in range(nc_):
                    strips.append(
                        (w_hh[r * 128:(r + 1) * 128, c * 2048:(c + 1) * 2048],
                         s_hh[r * 128:(r + 1) * 128, c * 2048:(c + 1) * 2048]))
        N_STRIPS = len(strips)
        pp = {"i": 0, "sign_ev": [None, None], "out_ev": [None, None]}
        pp_outs = [None] * N_STRIPS
        pp_off = [PP0, PP1]

        def pump(n):
            for _ in range(n):
                i = pp["i"]
                if i >= N_STRIPS:
                    return
                pp["i"] = i + 1
                par = i % 2
                off = pp_off[par]
                src, dst = strips[i]
                k.waits("gpsimd", [pp["out_ev"][par]])
                ev_in = gq.batch([(ph3b[:, off:off + 2048], src)])
                k.waits("scalar", [ev_in])
                ev_sg = a_inc(lambda e, o=off: e.sign(
                    ph3b[:, o:o + 2048], ph3b[:, o:o + 2048]))
                pp["sign_ev"][par] = ev_sg
                k.waits("scalar", [ev_sg])
                pp["out_ev"][par] = scq.batch([(dst,
                                                 ph3b[:, off:off + 2048])])
                pp_outs[i] = pp["out_ev"][par]

        wh_consumed = [None, None]

        def load_half(par, nm, half):
            k.waits("scalar", [wh_consumed[par], swx_ev[nm]])
            items = []
            for gblk in range(4):
                o3 = ph2[par][:, gblk * 4096:(gblk + 1) * 4096].rearrange(
                    "p (a c) -> p a c", a=4)
                i3 = swx[nm][gblk * 512:(gblk + 1) * 512,
                             half * 1024:(half + 1) * 1024]
                i3t = i3.rearrange("(a q) c -> a q c", a=4).transpose([1, 0, 2])
                items.append((o3, i3t))
            return scq.batch(items)

        HW = ["wq", "wk", "wb", "wv", "wo"]
        half_sched = [(t, widx, half) for t in range(NTA)
                      for widx in range(5) for half in (0, 1)]
        half_evs, half_par = {}, {}
        hs = {"i": 0}

        def issue_halves(n):
            for _ in range(n):
                i = hs["i"]
                if i >= len(half_sched):
                    return
                hs["i"] = i + 1
                t, widx, half = half_sched[i]
                par = i % 2
                half_evs[(t, widx, half)] = load_half(par, HW[widx], half)
                half_par[(t, widx, half)] = par

        rot_free = [None] * 8
        stage_ev = [None] * 4
        beta_ev = {}
        att_sb_free = [None, None]
        attbank_free = [None, None]
        obank_free = [None, None]
        state_evac_ev = [None] * 4
        state_mm_ev = [None] * 4
        lnin_free = None
        outst_free = None
        ln1_free = [None, None]
        stg_ev = None
        ln1_stage_ev = None
        xfm_free = None
        qfm_free = None
        kfm_free = None
        ktok_ufree = None
        prev_qtrr = None
        prev_ktrr = None
        stream_done = [None] * NTA
        ev_q_trr = [None] * NH
        ev_k_trr = [None] * NH

        issue_halves(2)

        def proj_tm(t, widx, evac):
            for half in (0, 1):
                par = half_par[(t, widx, half)]
                lev = half_evs[(t, widx, half)]
                for fgl in range(2):
                    fg = half * 2 + fgl
                    for tb in range(NTB):
                        bi = 4 + (tb % 2) * 2 + fgl
                        pairs = []
                        for dc in range(NDC):
                            lhsT = x_fm[:, dc * 512 + tb * 128:
                                        dc * 512 + (tb + 1) * 128]
                            rhs = ph2[par][:, dc * 1024 + fgl * 512:
                                           dc * 1024 + (fgl + 1) * 512]
                            pairs.append((lhsT, rhs))
                        ev = mm_group(pb[bi][:, :], pairs,
                                      extra_wait=[lev, rot_free[bi]])
                        rot_free[bi] = evac(tb, fg, bi, ev)
                wh_consumed[par] = t_quiesce()
                issue_halves(1)

        def emit_quant_tb(f, tb, use_wdb=False):
            frow = f * TF
            geb = None if use_wdb else ph3a[:, 0:2048]
            k.waits("gpsimd", stream_done[f] + [self_outst_free[0]])
            ld_ev = gq.batch([(outst[:, :],
                               sstream[frow + tb * 128:
                                       frow + (tb + 1) * 128, :])])
            ev_nmi, inv_ap, nmi_ap = ln_stats(
                lambda i: outst[:, i * 512:(i + 1) * 512],
                eps_rms, [ld_ev, self_lnin_free[0], self_tmpf_free[0],
                          self_geb_gate[0]], base=0)
            k.waits("scalar", [ev_nmi, self_lnin_free[0]])
            ev_rm = a_inc(lambda e, sc_=inv_ap, bi_=nmi_ap: e.activation(
                lnin[:, :], outst[:, :], AF.Identity, bias=bi_, scale=sc_))
            k.waits("vector", [ev_rm])
            k.on("vector", lambda e: e.tensor_reduce(
                stats[:, 80:81], lnin[:, :], axis=AX.X, op=ALU.max,
                apply_absolute_value=True))
            k.drain("vector")
            k.on("vector", lambda e: e.reciprocal(stats[:, 81:82],
                                                  stats[:, 80:81]))
            k.drain("vector")
            k.on("vector", lambda e: e.tensor_scalar_mul(
                stats[:, 82:83], stats[:, 81:82], 127.0))
            k.on("vector", lambda e: e.tensor_scalar_mul(
                stats[:, 83:84], stats[:, 80:81], 1.0 / 127.0))
            k.drain("vector")
            ev_qs = v_inc(lambda e: e.tensor_scalar_mul(
                outst[:, :], lnin[:, :], stats[:, 82:83]))
            k.waits("scalar", [ev_qs])
            k.on("scalar", lambda e: e.sign(lnin[:, :], outst[:, :]))
            ev_ab = a_inc(lambda e: e.activation(tmpf[:, :], outst[:, :],
                                                 AF.Abs))
            k.waits("vector", [ev_ab])
            k.on("vector", lambda e: e.tensor_scalar_add(
                tmpf[:, :], tmpf[:, :], 0.5))
            k.drain("vector")
            k.on("vector", lambda e: e.tensor_copy(outst[:, :], tmpf[:, :]))
            k.drain("vector")
            for kbit in (64.0, 32.0, 16.0, 8.0, 4.0, 2.0, 1.0):
                if use_wdb:
                    for j in range(4):
                        k.on("vector", lambda e, kb=kbit, j_=j:
                             e.tensor_scalar(
                                 wdb[j_][:, :],
                                 tmpf[:, j_ * 512:(j_ + 1) * 512], kb, kb,
                                 op0=ALU.is_ge, op1=ALU.mult))
                    k.drain("vector")
                    for j in range(4):
                        k.on("vector", lambda e, j_=j: e.tensor_tensor(
                            tmpf[:, j_ * 512:(j_ + 1) * 512],
                            tmpf[:, j_ * 512:(j_ + 1) * 512],
                            wdb[j_][:, :], op=ALU.subtract))
                    k.drain("vector")
                else:
                    k.on("vector", lambda e, kb=kbit: e.tensor_scalar(
                        geb, tmpf[:, :], kb, kb, op0=ALU.is_ge, op1=ALU.mult))
                    k.drain("vector")
                    k.on("vector", lambda e: e.tensor_tensor(
                        tmpf[:, :], tmpf[:, :], geb, op=ALU.subtract))
                    k.drain("vector")
            k.on("vector", lambda e: e.tensor_tensor(
                tmpf[:, :], outst[:, :], tmpf[:, :], op=ALU.subtract))
            k.drain("vector")
            k.on("vector", lambda e: e.tensor_scalar_min(
                tmpf[:, :], tmpf[:, :], 127.0))
            k.drain("vector")
            ev_xq = v_inc(lambda e: e.scalar_tensor_tensor(
                outst[:, :], lnin[:, :], stats[:, 83:84], tmpf[:, :],
                op0=ALU.mult, op1=ALU.mult))
            self_lnin_free[0] = ev_xq
            self_tmpf_free[0] = ev_xq
            self_outst_free[0] = ev_xq
            k.waits("gpsimd", [ev_xq])
            wr = gq.batch([(sxq[frow + tb * 128:frow + (tb + 1) * 128, :],
                            outst[:, :])])
            self_outst_free[0] = wr
            return wr, (sV.h, sV.n)

        def emit_xq_trr(f, xq_wr, extra):
            frow = f * TF
            k.waits("sync", xq_wr + extra)
            return syq.batch(
                [(xq_fm[:, dc * 512:(dc + 1) * 512],
                  sxq[frow:frow + TF, dc * 128:(dc + 1) * 128])
                 for dc in range(NDC)], transpose=True)

        self_outst_free = [None]
        self_lnin_free = [None]
        self_tmpf_free = [None]
        self_geb_gate = [None]

        hoisted_xfm = None
        hq_wr = []
        hq_done = None
        for t in range(NTA):
            trow = t * TA

            if hoisted_xfm is not None:
                ev_xfm = hoisted_xfm
            else:
                k.waits("gpsimd", [xfm_free])
                ev_xfm = gq.batch(
                    [(x_fm[:, dc * 512:(dc + 1) * 512],
                      xt_h[dc * 128:(dc + 1) * 128, trow:trow + TA])
                     for dc in range(NDC)])

            def q_evac(tb, fg, bi, mm_ev):
                j = (tb * 4 + fg) % 4
                sl = ph3b[:, SS + j * 512:SS + (j + 1) * 512]
                k.waits("vector", [mm_ev, stage_ev[j]])
                v_ev = v_inc(lambda e, o=sl, i=pb[bi][:, :]:
                             e.tensor_copy(o, i))
                k.waits("scalar", [v_ev, prev_qtrr])
                stage_ev[j] = scq.batch(
                    [(sq[tb * 128:(tb + 1) * 128,
                         fg * 512:(fg + 1) * 512], sl)])
                return v_ev

            k.waits("tensor", [ev_xfm])
            proj_tm(t, 0, q_evac)
            if t >= 2:
                self_lnin_free[0] = ln1_free[0]
                self_tmpf_free[0] = ln1_free[1]
                for tb_h in ((0, 1) if t == 2 else (2, 3)):
                    wr, hq_done = emit_quant_tb(0, tb_h, use_wdb=True)
                    hq_wr.append(wr)
                ln1_free[0] = hq_done
                ln1_free[1] = hq_done
            q_spills = list(stage_ev)
            pump(10)

            k.waits("sync", q_spills + [qfm_free])
            ev_qtrr_all = syq.batch(
                [(q_fm[:, h * 512:(h + 1) * 512], sq[:, h * 128:(h + 1) * 128])
                 for h in range(NH)], transpose=True)
            prev_qtrr = ev_qtrr_all

            def k_evac(tb, fg, bi, mm_ev):
                sl = k_tok[:, tb * 2048 + fg * 512:tb * 2048 + (fg + 1) * 512]
                k.waits("vector", [mm_ev, ktok_ufree])
                v_ev = v_inc(lambda e, o=sl, i=pb[bi][:, :]:
                             e.tensor_copy(o, i))
                k.waits("scalar", [v_ev, prev_ktrr])
                k_spill_evs.append(scq.batch(
                    [(sk[tb * 128:(tb + 1) * 128,
                         fg * 512:(fg + 1) * 512], sl)]))
                return v_ev

            k_spill_evs = []
            proj_tm(t, 1, k_evac)
            pump(10)

            k.waits("sync", k_spill_evs + [kfm_free])
            ev_ktrr_all = syq.batch(
                [(k_fm[:, h * 512:(h + 1) * 512], sk[:, h * 128:(h + 1) * 128])
                 for h in range(NH)], transpose=True)
            prev_ktrr = ev_ktrr_all

            def b_evac(tb, fg, bi, mm_ev):
                sl = u_tok[:, tb * 2048 + fg * 512:tb * 2048 + (fg + 1) * 512]
                k.waits("scalar", [mm_ev, ktok_ufree])
                ev = a_inc(lambda e, o=sl, i=pb[bi][:, :]:
                           e.activation(o, i, AF.Sigmoid))
                beta_ev[(tb, fg)] = ev
                return ev

            proj_tm(t, 2, b_evac)
            pump(10)

            def v_evac(tb, fg, bi, mm_ev):
                sl = u_tok[:, tb * 2048 + fg * 512:tb * 2048 + (fg + 1) * 512]
                k.waits("vector", [mm_ev, beta_ev[(tb, fg)]])
                return v_inc(lambda e, o=sl, i=pb[bi][:, :]:
                             e.tensor_tensor(o, o, i, op=ALU.mult))

            proj_tm(t, 3, v_evac)
            u_done = (sV.h, sV.n)
            xfm_free = t_quiesce()
            pump(10)
            if t + 1 < NTA:
                k.waits("gpsimd", [xfm_free])
                hoisted_xfm = gq.batch(
                    [(x_fm[:, dc * 512:(dc + 1) * 512],
                      xt_h[dc * 128:(dc + 1) * 128,
                           (t + 1) * TA:(t + 2) * TA])
                     for dc in range(NDC)])

            for c in range(CPT):
                g = t * CPT + c
                boff = (c % 2) * 64
                tb = c // 2
                mask_evs = [None, None]
                for hb in range(2):
                    bi = 4 + hb
                    k.waits("tensor", [attbank_free[hb], rot_free[4 + hb],
                                       ev_ktrr_all, ev_qtrr_all])
                    p1_ev = None
                    for hh in range(8):
                        h = hb * 8 + hh
                        l = k_fm[:, h * 512 + c * 64:h * 512 + (c + 1) * 64]
                        r = q_fm[:, h * 512 + c * 64:h * 512 + (c + 1) * 64]
                        o = pb[bi][boff:boff + 64, hh * 64:(hh + 1) * 64]
                        p1_ev = t_inc(lambda e, oo=o, ll=l, rr=r: e.matmul(
                            oo, ll, rr, start=True, stop=True,
                            skip_group_check=True))
                    ato = AT0 if hb == 0 else AT1
                    k.waits("vector", [p1_ev, att_sb_free[hb], ev_mask,
                                       ln1_stage_ev])
                    mask_evs[hb] = v_inc(
                        lambda e, o=ph3b[boff:boff + 64, ato:ato + 512],
                        i=pb[bi][boff:boff + 64, :],
                        m=maskb[boff:boff + 64, :]:
                        e.tensor_tensor(o, i, m, op=ALU.mult))
                    attbank_free[hb] = mask_evs[hb]
                for hb in range(2):
                    obi = 6 + hb
                    ato = AT0 if hb == 0 else AT1
                    k.waits("tensor", [mask_evs[hb], obank_free[hb],
                                       rot_free[6 + hb], u_done])
                    last_o = None
                    for hh in range(8):
                        h = hb * 8 + hh
                        sbk = h // 4
                        u_l = u_tok[boff:boff + 64,
                                    tb * 2048 + h * 128:
                                    tb * 2048 + (h + 1) * 128]
                        at_r = ph3b[boff:boff + 64,
                                    ato + hh * 64:ato + (hh + 1) * 64]
                        o_ap = pb[obi][:, hh * 64:(hh + 1) * 64]
                        k.waits("tensor", [state_evac_ev[sbk]])
                        if g > 0:
                            k.on("tensor", lambda e, oo=o_ap, ll=u_l, rr=at_r:
                                 e.matmul(oo, ll, rr, start=True, stop=False,
                                          skip_group_check=True))
                            st_l = state_sb[:, h * 128:(h + 1) * 128]
                            q_r = q_fm[:, h * 512 + c * 64:
                                       h * 512 + (c + 1) * 64]
                            last_o = t_inc(
                                lambda e, oo=o_ap, ll=st_l, rr=q_r: e.matmul(
                                    oo, ll, rr, start=False, stop=True,
                                    skip_group_check=True))
                        else:
                            last_o = t_inc(
                                lambda e, oo=o_ap, ll=u_l, rr=at_r: e.matmul(
                                    oo, ll, rr, start=True, stop=True,
                                    skip_group_check=True))
                        kt_l = k_tok[boff:boff + 64,
                                     tb * 2048 + h * 128:
                                     tb * 2048 + (h + 1) * 128]
                        st_o = pb[sbk][:, (h % 4) * 128:(h % 4 + 1) * 128]
                        a0 = (g == 0 and h % 4 == 0)
                        b0 = (g == NCHUNK - 1 and h % 4 == 3)
                        if h % 4 == 3:
                            state_mm_ev[sbk] = t_inc(
                                lambda e, oo=st_o, ll=kt_l, rr=u_l, a=a0,
                                b=b0: e.matmul(oo, ll, rr, start=a, stop=b,
                                               skip_group_check=True))
                        else:
                            k.on("tensor",
                                 lambda e, oo=st_o, ll=kt_l, rr=u_l, a=a0,
                                 b=b0: e.matmul(oo, ll, rr, start=a, stop=b,
                                                skip_group_check=True))
                    k.waits("vector", [last_o])
                    oq = q_fm[:, :].rearrange("p (h s) -> p h s", h=NH)
                    opv = pb[obi][:, :].rearrange("p (h s) -> p h s", h=8)
                    oev = v_inc(lambda e,
                                o=oq[:, hb * 8:(hb + 1) * 8,
                                     c * 64:(c + 1) * 64],
                                i=opv: e.tensor_copy(o, i))
                    obank_free[hb] = oev
                chunk_q = t_quiesce()
                att_sb_free[0] = chunk_q
                att_sb_free[1] = chunk_q
                if g < NCHUNK - 1:
                    for sbk in range(4):
                        k.waits("vector", [state_mm_ev[sbk]])
                        state_evac_ev[sbk] = v_inc(
                            lambda e,
                            o=state_sb[:, sbk * 512:(sbk + 1) * 512],
                            i=pb[sbk][:, :]: e.tensor_copy(o, i))
            scan_o_done = (sV.h, sV.n)
            scan_mm_done = chunk_q
            ktok_ufree = scan_mm_done
            kfm_free = scan_mm_done
            pump(8)

            stream_evs = []
            xld = [None] * NTB
            for tb in (0, 1):
                par2 = tb % 2
                buf = lnin if par2 == 0 else tmpf
                k.waits("scalar", [ln1_free[par2]])
                xld[tb] = scq.batch(
                    [(buf[:, :],
                      x_h[trow + tb * 128:trow + (tb + 1) * 128, :])])
            for tb in range(NTB):
                par2 = tb % 2
                buf = lnin if par2 == 0 else tmpf
                if xld[tb] is None:
                    k.waits("scalar", [ln1_free[par2]])
                    xld[tb] = scq.batch(
                        [(buf[:, :],
                          x_h[trow + tb * 128:trow + (tb + 1) * 128, :])])
                evac_evs = []
                for dg in range(4):
                    half = dg // 2
                    dgl = dg % 2
                    par = half_par[(t, 4, half)]
                    lev = half_evs[(t, 4, half)]
                    bi = 4 + (tb % 2) * 2 + dgl
                    pairs = []
                    for fc in range(NDC):
                        lhsT = q_fm[:, fc * 512 + tb * 128:
                                    fc * 512 + (tb + 1) * 128]
                        rhs = ph2[par][:, fc * 1024 + dgl * 512:
                                       fc * 1024 + (dgl + 1) * 512]
                        pairs.append((lhsT, rhs))
                    ev = mm_group(pb[bi][:, :], pairs,
                                  extra_wait=[lev, rot_free[bi], scan_o_done])
                    k.waits("vector", [ev, xld[tb]])
                    eev = v_inc(lambda e, o=buf[:, dg * 512:(dg + 1) * 512],
                                i=pb[bi][:, :]: e.tensor_tensor(
                                    o, o, i, op=ALU.add))
                    rot_free[bi] = eev
                    evac_evs.append(eev)
                ev_nmi, inv_ap, nmi_ap = ln_stats(
                    lambda i, b_=buf: b_[:, i * 512:(i + 1) * 512], eps_ln,
                    evac_evs, base=36 * par2)
                stgb = ph3b[:, 6144:8192]
                k.waits("scalar", [ev_nmi, stg_ev])
                ap_ev = a_inc(lambda e, o=stgb, b_=buf, sc_=inv_ap,
                              bi_=nmi_ap: e.activation(
                                  o, b_[:, :], AF.Identity, bias=bi_,
                                  scale=sc_))
                ln1_free[par2] = ap_ev
                k.waits("scalar", [ap_ev])
                stg_ev = scq.batch(
                    [(sstream[trow + tb * 128:trow + (tb + 1) * 128, :],
                      stgb)])
                stream_evs.append(stg_ev)
            stream_done[t] = stream_evs
            ln1_stage_ev = stg_ev
            wo_q = t_quiesce()
            qfm_free = wo_q
            wh_consumed[half_par[(t, 4, 0)]] = wo_q
            wh_consumed[half_par[(t, 4, 1)]] = wo_q
            issue_halves(2)
            pump(8)

        pump(N_STRIPS)
        tdone_list = scq.qevents()
        ppd_list = tdone_list
        attn_mm = wo_q
        attn_v = (sV.h, sV.n)
        attn_a = (sA.h, sA.n)

        p_sl = lambda fc: ph2[fc // 32][:, (fc % 32) * 512:(fc % 32 + 1) * 512]
        stream_sl = lambda tb: ph3b[:, tb * 2048:(tb + 1) * 2048]

        span_free = [None, None]
        pref_spans = {}
        dbg_evs = []
        quant_init = False
        wdb_free = [None] * 4
        y_ev = None
        prev_ln2_use = None
        prev_xq_trr = None
        prev_xq_wr = None
        prev_u_done = None
        prev_wd_read = None
        ln2_usage = None

        self_lnin_free[0] = ln1_free[0]
        self_tmpf_free[0] = ln1_free[1]
        for f in range(NTF):
            frow = f * TF

            if f == 0:
                self_geb_gate[0] = attn_mm
                ev_xq_trr = emit_xq_trr(0, hq_wr, [attn_mm, hq_done])

            stream_ld = [None] * NTB
            for tb in range(NTB):
                k.waits("gpsimd", stream_done[f] +
                        [prev_wd_read, prev_ln2_use] +
                        (tdone_list + [attn_v, attn_mm, attn_a]
                         if f == 0 else []))
                stream_ld[tb] = gq.batch(
                    [(stream_sl(tb),
                      sstream[frow + tb * 128:frow + (tb + 1) * 128, :])])

            def load_span(par, w_sT, sidx, first_pair, phase_extra=None):
                extra = []
                if f == 0 and first_pair:
                    extra.extend(tdone_list)
                if first_pair:
                    extra.append(prev_u_done)
                    if phase_extra is not None:
                        extra.append(phase_extra)
                a, b = ph1[par * 2], ph1[par * 2 + 1]
                k.waits("sync", [span_free[par]] + extra)
                k.waits("scalar", [span_free[par]] + extra)
                it_s, it_a = [], []
                for dc in range(NDC):
                    dst = a if dc < 8 else b
                    dsl = dst[:, (dc % 8) * 1024:(dc % 8 + 1) * 1024]
                    src = w_sT[dc * 128:(dc + 1) * 128,
                               sidx * 1024:(sidx + 1) * 1024]
                    (it_s if dc % 2 == 0 else it_a).append((dsl, src))
                evs = syq.batch(it_s)
                evsc = scq.batch(it_a)
                return (evs, evsc)

            gu_rot = [None, None, None]

            def gu_phase(w_s, is_g, phase_extra=None):
                for sidx in range(8):
                    par = sidx % 2
                    if is_g and sidx < 2 and pref_spans.get(sidx) is not None:
                        lev = pref_spans.pop(sidx)
                    else:
                        lev = load_span(par, w_s, sidx, sidx < 2, phase_extra)
                    a, b = ph1[par * 2], ph1[par * 2 + 1]
                    last = None
                    for jfc in range(8):
                        fc = sidx * 8 + jfc
                        bi = jfc % 3
                        pairs = []
                        for dc in range(NDC):
                            src = a if dc < 8 else b
                            lhsT = src[:, (dc % 8) * 1024 + jfc * 128:
                                       (dc % 8) * 1024 + (jfc + 1) * 128]
                            rhs = xq_fm[:, dc * 512:(dc + 1) * 512]
                            pairs.append((lhsT, rhs))
                        ev = mm_group(pb[bi][:, :], pairs,
                                      extra_wait=[lev[0], lev[1], gu_rot[bi],
                                                  ev_xq_trr])
                        if is_g:
                            k.waits("scalar", [ev])
                            gu_rot[bi] = a_inc(
                                lambda e, o=p_sl(fc), i=pb[bi][:, :]:
                                e.activation(o, i, AF.Silu))
                        else:
                            k.waits("vector", [ev, gu_rot[bi]])
                            gu_rot[bi] = v_inc(
                                lambda e, o=p_sl(fc), i=pb[bi][:, :]:
                                e.tensor_tensor(o, o, i, op=ALU.mult))
                        last = ev
                    span_free[par] = t_quiesce()

            gu_phase(swgT, True)
            g_done_a = (sA.h, sA.n)
            gu_phase(swuT, False, phase_extra=g_done_a)
            u_done_v = (sV.h, sV.n)
            u_mm_done = t_quiesce()
            prev_u_done = u_mm_done
            if f + 1 < NTF:
                self_geb_gate[0] = u_mm_done
                next_xq_wr = []
            if dbg and f == NTF - 1:
                k.waits("scalar", [u_done_v, ev_xq_trr])
                dbg_evs.append(scq.batch([(dbg_p[:, 0:16384], ph2[0][:, :])]))
                dbg_evs.append(scq.batch([(dbg_p[:, 16384:32768],
                                           ph2[1][:, :])]))
                dbg_evs.append(scq.batch([(dbg_xqf[:, :], ph3a[:, :])]))

            for dg in range(4):
                if f + 1 < NTF:
                    wr, qd = emit_quant_tb(f + 1, dg)
                    next_xq_wr.append(wr)
                    next_quant_done = qd
                for fc in range(NFC):
                    par4 = fc % 4
                    qeng = "sync" if fc % 2 == 0 else "scalar"
                    qq = syq if fc % 2 == 0 else scq
                    k.waits(qeng, [wdb_free[par4]])
                    lev = qq.batch(
                        [(wdb[par4][:, :],
                          swdT[fc * 128:(fc + 1) * 128,
                               dg * 512:(dg + 1) * 512])])
                    wm_extra = [lev, u_done_v, g_done_a] if fc == 0 else [lev]
                    k.waits("tensor", wm_extra)
                    for tb in range(NTB):
                        bi = 4 + tb
                        if fc == 0:
                            k.waits("tensor", [rot_free[bi]])
                        o = pb[bi][:, :]
                        l = p_sl(fc)[:, tb * 128:(tb + 1) * 128]
                        r = wdb[par4][:, :]
                        ev = t_inc(lambda e, oo=o, ll=l, rr=r,
                                   a=(fc == 0), b=(fc == NFC - 1):
                                   e.matmul(oo, ll, rr, start=a, stop=b,
                                            skip_group_check=True))
                        if tb == NTB - 1:
                            wdb_free[par4] = ev
                wd_done = (sT.h, sT.n)
                for tb in range(NTB):
                    bi = 4 + tb
                    k.waits("vector", [wd_done, u_mm_done, u_done_v,
                                       stream_ld[tb]])
                    ev = v_inc(lambda e,
                               o=stream_sl(tb)[:, dg * 512:(dg + 1) * 512],
                               i=pb[bi][:, :]:
                               e.tensor_tensor(o, o, i, op=ALU.add))
                    rot_free[bi] = ev
            ln2_ready = (sV.h, sV.n)
            prev_wd_read = ln2_ready
            if f + 1 < NTF:
                next_xq_trr = emit_xq_trr(f + 1, next_xq_wr,
                                          [next_quant_done])
                for sidx in (0, 1):
                    pref_spans[sidx] = load_span(sidx % 2, swgT, sidx, True)

            for tb in range(NTB):
                par2 = tb % 2
                buf = outst if par2 == 0 else tmpf
                buf_free = self_outst_free if par2 == 0 else self_tmpf_free
                ev_nmi, inv_ap, nmi_ap = ln_stats(
                    lambda i, tb_=tb: stream_sl(tb_)[:, i * 512:(i + 1) * 512],
                    eps_ln, [ln2_ready, buf_free[0]], base=36 * par2)
                k.waits("scalar", [ev_nmi, buf_free[0]])
                ap_ev = a_inc(lambda e, tb_=tb, o=buf, sc_=inv_ap,
                              bi_=nmi_ap: e.activation(
                                  o[:, :], stream_sl(tb_), AF.Identity,
                                  bias=bi_, scale=sc_))
                ln2_usage = ap_ev
                prev_ln2_use = ap_ev
                buf_free[0] = ap_ev
                k.waits("scalar", [ap_ev])
                y_ev = scq.batch(
                    [(y_h[frow + tb * 128:frow + (tb + 1) * 128, :],
                      buf[:, :])])
                y_evs.append(y_ev)
                buf_free[0] = y_ev
            if f + 1 < NTF:
                ev_xq_trr = next_xq_trr

        k.waits("sync", y_evs + dbg_evs)

        with nc.Block() as block:
            @block.sync
            def _(e):
                for fn in k.items["sync"]:
                    fn(e)

            @block.scalar
            def _(e):
                for fn in k.items["scalar"]:
                    fn(e)

            @block.tensor
            def _(e):
                for fn in k.items["tensor"]:
                    fn(e)

            @block.vector
            def _(e):
                for fn in k.items["vector"]:
                    fn(e)

            @block.gpsimd
            def _(e):
                for fn in k.items["gpsimd"]:
                    fn(e)

    return nc


_MASK = None


def _mask_const():
    global _MASK
    if _MASK is None:
        tri = np.triu(np.ones((64, 64), dtype=np.float32))
        _MASK = np.ascontiguousarray(np.tile(tri, (2, 8)))
    return _MASK


def kernel(L0_attn_hidden_states, L0_attn_Wq, L0_attn_Wk, L0_attn_Wv,
           L0_attn_Wb, L0_attn_Wo, L0_ffn_W_g, L0_ffn_W_u, L0_ffn_W_d):
    x = np.ascontiguousarray(np.asarray(L0_attn_hidden_states,
                                        dtype=np.float32))
    ws = {
        "wq": np.ascontiguousarray(np.asarray(L0_attn_Wq, dtype=np.float32)),
        "wk": np.ascontiguousarray(np.asarray(L0_attn_Wk, dtype=np.float32)),
        "wv": np.ascontiguousarray(np.asarray(L0_attn_Wv, dtype=np.float32)),
        "wb": np.ascontiguousarray(np.asarray(L0_attn_Wb, dtype=np.float32)),
        "wo": np.ascontiguousarray(np.asarray(L0_attn_Wo, dtype=np.float32)),
        "wgt": np.ascontiguousarray(np.asarray(L0_ffn_W_g, dtype=np.float32).T),
        "wut": np.ascontiguousarray(np.asarray(L0_ffn_W_u, dtype=np.float32).T),
        "wdt": np.ascontiguousarray(np.asarray(L0_ffn_W_d, dtype=np.float32).T),
        "maskb": _mask_const(),
    }
    nc = build()
    in_maps = []
    for b in range(8):
        m = {"x": np.ascontiguousarray(x[b]),
             "xt": np.ascontiguousarray(x[b].T)}
        m.update(ws)
        in_maps.append(m)
    res = run_bass_kernel_spmd(nc, in_maps, core_ids=list(range(8)))
    return np.stack([r["y"] for r in res.results], axis=0)
